# version 09
# baseline (speedup 1.0000x reference)
from contextlib import ExitStack

import numpy as np
import ml_dtypes
import concourse.bass as bass
import concourse.mybir as mybir
from concourse.bass_utils import run_bass_kernel_spmd
import concourse.bass_utils as _bu

if not getattr(_bu, "_ldw_patched", False):
    _orig_run_command = _bu.run_command

    def _run_command_ldw(argv, **kw):
        argv = ["--enable-ldw-opt=true" if a == "--enable-ldw-opt=false" else a
                for a in argv]
        return _orig_run_command(argv, **kw)

    _bu.run_command = _run_command_ldw
    _bu._ldw_patched = True

F32 = mybir.dt.float32
BF16 = mybir.dt.bfloat16
AF = mybir.ActivationFunctionType
ALU = mybir.AluOpType
AX = mybir.AxisListType

N_CORES = 8
SEQ = 2048
D = 1024
NSEQ = 4
NT = SEQ // 128
NH = 16
DH = 64
NE = 8
FF = 1024
EPS = 1e-5
BETA = 0.125


def _prod(xs):
    r = 1
    for v in xs:
        r *= v
    return r


class Prog:

    CENG = ("pe", "act", "dve")

    def __init__(self, nc, ctx):
        self.nc = nc
        self.ctx = ctx
        self.stream = {e: [] for e in ("pe", "act", "dve", "pool", "sync")}
        self.sems = {}
        self.cnt = {}
        self.waited = {e: {} for e in self.stream}
        self.track = {}
        self.engsem = {}
        self.dmasem = {}
        self.psum_names = set()
        self.arena_names = set()
        self.last_barrier = []
        self.out_waits = {}
        self.dma_lanes = {}
        self.dma_lane_sems = set()

    def _mksem(self, name):
        self.sems[name] = self.ctx.enter_context(self.nc.semaphore(name))
        self.cnt[name] = 0
        return name

    def new_tag(self, tag):
        for e in ("pe", "act", "dve", "pool"):
            self.engsem[e] = self._mksem(f"{e}_{tag}")

    def lane(self, key):
        if key not in self.dma_lanes:
            name = self._mksem(f"dl_{key}")
            self.dma_lanes[key] = name
            self.dma_lane_sems.add(name)
        return self.dma_lanes[key]

    def _box(self, ap):
        t = ap.tensor
        name = t.name
        if name in self.psum_names:
            dims = [list(d) for d in ap.ap]
            off = ap.offset
            shape = list(t.shape)
            F = _prod(shape[1:]) if len(shape) > 1 else 1
            rem = off % F if F else 0
            lo = hi = rem
            for s, c in dims[1:]:
                d = s * (c - 1)
                if d >= 0:
                    hi += d
                else:
                    lo += d
            b0 = (lo * 4) // 2048
            b1 = ((hi * 4) // 2048) + 1
            return (name, 0, 1, b0, b1)
        dims = [list(d) for d in ap.ap]
        off = ap.offset
        tname = type(t).__name__
        if "DRam" in tname:
            lo = hi = off
            for s, c in dims:
                d = s * (c - 1)
                if d >= 0:
                    hi += d
                else:
                    lo += d
            return (name, 0, 1, lo, hi + 1)
        shape = list(t.shape)
        F = _prod(shape[1:]) if len(shape) > 1 else 1
        s0, c0 = dims[0]
        ok = F > 0 and s0 != 0 and s0 % F == 0
        if ok:
            for s, c in dims[1:]:
                if c > 1 and abs(s) * (c - 1) >= F:
                    ok = False
                    break
        if not ok:
            return (name, 0, 1 << 20, 0, 1 << 40)
        p0 = off // F
        pstep = s0 // F
        p1 = p0 + pstep * (c0 - 1) + 1
        rem = off % F
        lo = hi = rem
        for s, c in dims[1:]:
            d = s * (c - 1)
            if d >= 0:
                hi += d
            else:
                lo += d
        return (name, p0, p1, lo, hi + 1)

    @staticmethod
    def _ovl(a, b):
        return a[0] == b[0] and not (
            a[2] <= b[1] or b[2] <= a[1] or a[4] <= b[3] or b[4] <= a[3]
        )

    @staticmethod
    def _covers(a, b):
        return (
            a[0] == b[0]
            and a[1] <= b[1]
            and a[2] >= b[2]
            and a[3] <= b[3]
            and a[4] >= b[4]
        )

    def emit(self, eng, fn, reads=(), writes=(), dma=None, out_dma=False):
        waits = {}

        def need(sem, val):
            if sem in self.dma_lane_sems:
                val = self.cnt[sem]
            if val <= 0:
                return
            if self.waited[eng].get(sem, 0) >= val:
                return
            if waits.get(sem, 0) < val:
                waits[sem] = val

        rb = [self._box(ap) for ap in reads]
        wb = [self._box(ap) for ap in writes]
        for b in rb:
            ent = self.track.get(b[0])
            if ent:
                for ob, sem, val in ent["w"]:
                    if self._ovl(b, ob):
                        need(sem, val)
                if b[0] in self.psum_names:
                    for ob, sem, val in ent["r"]:
                        if self._ovl(b, ob):
                            need(sem, val)
        for b in wb:
            ent = self.track.get(b[0])
            if ent:
                for ob, sem, val in ent["w"]:
                    if self._ovl(b, ob):
                        need(sem, val)
                for ob, sem, val in ent["r"]:
                    if self._ovl(b, ob):
                        need(sem, val)
        if dma and any(b[0] in self.arena_names for b in wb):
            for sem, val in self.last_barrier:
                need(sem, val)
        semname = self.lane(dma) if dma else self.engsem[eng]
        amt = 16 if dma else 1
        self.cnt[semname] += amt
        val = self.cnt[semname]
        wl = sorted(waits.items())
        for s, v in wl:
            self.waited[eng][s] = max(self.waited[eng].get(s, 0), v)
        self.stream[eng].append((wl, fn, semname, amt))
        if out_dma:
            self.out_waits[semname] = val
        for b in wb:
            ent = self.track.setdefault(b[0], {"w": [], "r": []})
            ent["w"] = [e_ for e_ in ent["w"] if not self._covers(b, e_[0])]
            ent["r"] = [e_ for e_ in ent["r"] if not self._covers(b, e_[0])]
            ent["w"].append((b, semname, val))
        for b in rb:
            ent = self.track.setdefault(b[0], {"w": [], "r": []})
            ent["r"].append((b, semname, val))
        return val

    def emit_raw(self, eng, fn):
        semname = self.engsem[eng]
        self.cnt[semname] += 1
        self.stream[eng].append(([], fn, semname, 1))

    def barrier(self):
        snap = [(self.engsem[e], self.cnt[self.engsem[e]]) for e in self.CENG]
        snap = [(s, v) for s, v in snap if v > 0]
        for e in self.CENG:
            wl = []
            for s, v in snap:
                if self.waited[e].get(s, 0) < v and s != self.engsem[e]:
                    wl.append((s, v))
                    self.waited[e][s] = v
            if wl:
                self.stream[e].append((wl, None, None, 0))
        self.last_barrier = snap

    def final_waits(self):
        per_eng = {}
        for semname, val in self.out_waits.items():
            per_eng.setdefault("pool", []).append((semname, val))
        for e, wl in per_eng.items():
            wl = [(s, v) for s, v in wl if self.waited[e].get(s, 0) < v]
            if wl:
                self.stream[e].append((sorted(wl), None, None, 0))

    def install(self):
        def runner(eng):
            def _run(e):
                for wl, fn, semname, amt in self.stream[eng]:
                    for s, v in wl:
                        e.wait_ge(self.sems[s], v)
                    if fn is not None:
                        inst = fn(e)
                        inst.then_inc(self.sems[semname], amt)
            return _run

        with self.nc.Block() as block:
            block.tensor(runner("pe"))
            block.scalar(runner("act"))
            block.vector(runner("dve"))
            block.gpsimd(runner("pool"))
            block.sync(runner("sync"))


def build_program(n_seq=NSEQ, stop=99):
    nc = bass.Bass("TRN2", target_bir_lowering=False)

    x_h = nc.dram_tensor("x", [n_seq, SEQ, D], F32, kind="ExternalInput")
    out_h = nc.dram_tensor("out", [n_seq, SEQ, D], F32, kind="ExternalOutput")
    wq_h = nc.dram_tensor("wq_pre", [8, 128, 8, 128], F32, kind="ExternalInput")
    wk_h = nc.dram_tensor("wk_pre", [8, 128, 8, 128], F32, kind="ExternalInput")
    wv_h = nc.dram_tensor("wv", [D, D], F32, kind="ExternalInput")
    wo_h = nc.dram_tensor("wo", [D, D], F32, kind="ExternalInput")
    bo_h = nc.dram_tensor("bo", [1, D], F32, kind="ExternalInput")
    gw_h = nc.dram_tensor("gw_pre", [128, 8, 8], F32, kind="ExternalInput")
    gb_h = nc.dram_tensor("gb", [1, 8], F32, kind="ExternalInput")
    w1_h = nc.dram_tensor("w1_pre", [9, 8, 128, 8, 2, 128], F32, kind="ExternalInput")
    b1_h = nc.dram_tensor("b1r", [144, 128], F32, kind="ExternalInput")
    w2_h = nc.dram_tensor("w2_pre", [9, 8, 128, D], F32, kind="ExternalInput")
    rb2_h = nc.dram_tensor("rb2", [8, D], F32, kind="ExternalInput")
    sb2_h = nc.dram_tensor("sb2", [1, D], F32, kind="ExternalInput")

    ident_np = np.eye(128, dtype=np.float32)
    tri = np.zeros((128, 4, 512), dtype=ml_dtypes.bfloat16)
    for k in range(4):
        for p_ in range(128):
            q0 = k * 128 + p_
            tri[p_, k, q0:] = 1.0
    tri = tri.reshape(128, 2048)
    ones128_f = np.ones((1, 128), dtype=np.float32)
    ones128_b = np.ones((1, 128), dtype=ml_dtypes.bfloat16)
    ident_h = nc.inline_tensor(ident_np, name="identf")
    tri_h = nc.inline_tensor(tri, name="trimask")
    onesf_h = nc.inline_tensor(ones128_f, name="onesf")
    onesb_h = nc.inline_tensor(ones128_b, name="onesb")

    x = x_h.ap()
    out = out_h.ap()

    ctx = ExitStack()
    with ctx:
        RESERVE = 209920
        resv = nc.alloc_sbuf_tensor("manual_reserve", [128, RESERVE], mybir.dt.uint8)
        base = None
        for al in nc.m.functions[0].allocations:
            if isinstance(al, mybir.MemoryLocationSet) and al.memorylocations and                al.memorylocations[0].name == resv.name:
                base = al.memorylocations[0].addr
        assert base is not None
        off = [0]

        def sb(name, shape, dtype, align=64):
            nbytes = _prod(shape[1:]) * (2 if dtype == BF16 else 4)
            o = (off[0] + align - 1) // align * align
            t = nc.alloc_sbuf_tensor_at(name, shape, dtype, offset=base + o)
            off[0] = o + nbytes
            return t

        acc = sb("acc", [128, NT, D], F32)
        xnT = sb("xnT", [128, 8, SEQ], BF16)
        identf = sb("identf_sb", [128, 128], F32)
        trimask = sb("trimask_sb", [128, 2048], BF16)
        onesf = sb("onesf_sb", [1, 128], F32)
        onesb = sb("onesb_sb", [1, 128], BF16)
        biasT = sb("biasT", [128, 144], F32)
        rb2_sb = sb("rb2_sb", [8, D], BF16)
        bo_sb = sb("bo_sb", [1, D], BF16)
        sb2_sb = sb("sb2_sb", [1, D], BF16)
        gw_sb = sb("gw_sb", [128, 8, 8], F32)
        gb_sb = sb("gb_sb", [1, 8], F32)
        cw = sb("cw", [128, NT, 8], F32)
        cwT = sb("cwT", [8, SEQ], BF16)
        stats = sb("stats", [128, 128], F32)
        epsc = sb("epsc", [128, 1], F32)
        rrow = sb("rrow", [1, 512], F32)
        xnf = sb("xnf", [128, 2, 128], F32)
        wqb = sb("wqb", [128, 2, 8, 128], BF16)
        wkb = sb("wkb", [128, 2, 8, 128], BF16)

        arena0 = off[0]
        off[0] = arena0
        V_sb = sb("V_sb", [128, NT, NH, 65], BF16)
        QT = sb("QT", [128, SEQ], BF16)
        KT = sb("KT", [128, SEQ], BF16)
        attn_c = off[0]
        wv_sb = sb("wv_sb", [128, 8, D], BF16)
        xnrow = sb("xnrow", [128, D], F32)
        sq = sb("sq", [128, D], F32)
        b1raw = sb("b1raw", [128, 128], F32)
        b1raw2 = sb("b1raw2", [16, 128], F32)
        glog = sb("glog", [128, NT, 8], F32)
        top2 = sb("top2", [128, 6, NT, 8], F32)
        off[0] = attn_c
        AVT = sb("AVT", [128, 4, SEQ], BF16)
        expT = sb("expT", [128, 3, 1024], BF16)
        avf = sb("avf", [64, 512], F32)
        wo_sb = sb("wo_sb", [128, 4, D], BF16)
        off[0] = arena0
        gT = sb("gT", [128, 8, SEQ], BF16)
        w2b = sb("w2b", [128, 2, 8, D], BF16)
        w1b = sb("w1b", [128, 2, 8, 2, 128], BF16)
        sa = sb("sa", [128, SEQ], F32)
        ffn_end = off[0]
        assert ffn_end <= 209920, f"sbuf overflow {ffn_end}"

        pA = nc.alloc_psum_tensor("pA", [128, 2048], F32)
        pB = nc.alloc_psum_tensor("pB", [128, 2048], F32)

        p = Prog(nc, ctx)
        p.psum_names = {pA.name, pB.name}
        p.arena_names = {t.name for t in [
            V_sb, QT, KT, wv_sb, xnrow, sq, b1raw, b1raw2, glog, top2,
            AVT, expT, avf, wo_sb, gT, w2b, w1b, sa,
        ]}
        p.new_tag("pro")

        E = p.emit

        def dma(eng, lane, dst, src, out_dma=False):
            E(eng, lambda e, d=dst, s=src: e.dma_start(out=d, in_=s),
              reads=[src], writes=[dst], dma=lane, out_dma=out_dma)

        dma("sync", "const", identf[:, :], ident_h.ap())
        dma("sync", "const", trimask[:, :], tri_h.ap())
        dma("sync", "const", onesf[:, :], onesf_h.ap())
        dma("sync", "const", onesb[:, :], onesb_h.ap())
        dma("sync", "const", gw_sb[:, :, :], gw_h.ap())
        dma("sync", "const", gb_sb[:, :], gb_h.ap())
        dma("sync", "const", b1raw[:, :], b1_h.ap()[0:128, :])
        dma("sync", "const", b1raw2[:, :], b1_h.ap()[128:144, :])
        dma("pool", "bias", rb2_sb[:, :], rb2_h.ap())
        dma("pool", "bias", bo_sb[:, :], bo_h.ap())
        dma("pool", "bias", sb2_sb[:, :], sb2_h.ap())

        E("dve", lambda e: e.memset(epsc[:, :], EPS), writes=[epsc[:, :]])
        E("pe", lambda e: e.transpose(pA[0:128, 0:128], b1raw[:, :], identf[:, :]),
          reads=[b1raw[:, :], identf[:, :]], writes=[pA[0:128, 0:128]])
        E("dve", lambda e: e.tensor_copy(biasT[:, 0:128], pA[0:128, 0:128]),
          reads=[pA[0:128, 0:128]], writes=[biasT[:, 0:128]])
        E("pe", lambda e: e.transpose(pA[0:128, 512:528], b1raw2[:, :], identf[0:16, 0:16]),
          reads=[b1raw2[:, :], identf[:, :]], writes=[pA[0:128, 512:528]])
        E("dve", lambda e: e.tensor_copy(biasT[:, 128:144], pA[0:128, 512:528]),
          reads=[pA[0:128, 512:528]], writes=[biasT[:, 128:144]])

        p.barrier()

        for s in range(n_seq):
            for t in range(NT):
                dma("sync", f"xl{t}", acc[:, t, :],
                    x[s, t * 128:(t + 1) * 128, :])
            dma("pool", "wv", wv_sb[:, :, :],
                wv_h.ap().rearrange("(kc p) n -> p kc n", p=128))
            dma("pool", "wq0", wqb[:, 0, :, :], wq_h.ap()[0, :, :, :])
            dma("pool", "wk0", wkb[:, 0, :, :], wk_h.ap()[0, :, :, :])
            E("dve", lambda e: e.memset(V_sb[:, :, :, 64:65], 1.0),
              writes=[V_sb[:, :, :, 64:65]])

            s1a = stats[:, 0:16]
            s2a = stats[:, 16:32]
            ma = stats[:, 32:48]
            va = stats[:, 48:64]
            ra = stats[:, 64:80]
            mxa = stats[:, 80:96]
            for t in range(NT):
                at = acc[:, t, :]
                E("dve", lambda e, at=at, o=s1a[:, t:t + 1]: e.reduce_sum(
                    o, at, axis=AX.X), reads=[at], writes=[s1a[:, t:t + 1]])
                E("act", lambda e, at=at: e.activation(sq[:, :], at, AF.Square),
                  reads=[at], writes=[sq[:, :]])
                E("dve", lambda e, o=s2a[:, t:t + 1]: e.reduce_sum(
                    o, sq[:, :], axis=AX.X), reads=[sq[:, :]],
                  writes=[s2a[:, t:t + 1]])
            E("dve", lambda e: e.tensor_scalar(ma, s1a, 1.0 / D, None, op0=ALU.mult),
              reads=[s1a], writes=[ma])
            E("act", lambda e: e.activation(mxa, ma, AF.Square),
              reads=[ma], writes=[mxa])
            E("dve", lambda e: e.tensor_scalar(va, s2a, 1.0 / D, None, op0=ALU.mult),
              reads=[s2a], writes=[va])
            E("dve", lambda e: e.tensor_tensor(va, va, mxa, op=ALU.subtract),
              reads=[va, mxa], writes=[va])
            E("act", lambda e: e.activation(va, va, AF.Ln, bias=epsc[:, :]),
              reads=[va, epsc[:, :]], writes=[va])
            E("act", lambda e: e.activation(ra, va, AF.Exp, scale=-0.5),
              reads=[va], writes=[ra])
            for t in range(NT):
                at = acc[:, t, :]
                E("dve", lambda e, at=at, m=ma[:, t:t + 1], r=ra[:, t:t + 1]:
                  e.tensor_scalar(xnrow[:, :], at, m, r,
                                  op0=ALU.subtract, op1=ALU.mult),
                  reads=[at, ma[:, t:t + 1], ra[:, t:t + 1]], writes=[xnrow[:, :]])
                gate_ps = pA[0:128, 1024:1032]
                for kc in range(8):
                    xp = pA[0:128, 0:128] if kc % 2 == 0 else pA[0:128, 512:640]
                    xsrc = xnrow[:, kc * 128:(kc + 1) * 128]
                    E("pe", lambda e, xp=xp, xsrc=xsrc: e.transpose(
                        xp, xsrc, identf[:, :]),
                      reads=[xsrc, identf[:, :]], writes=[xp])
                    xf = xnf[:, kc % 2, :]
                    E("dve", lambda e, xp=xp, xf=xf: e.tensor_copy(xf, xp),
                      reads=[xp], writes=[xf])
                    xdst = xnT[:, kc, t * 128:(t + 1) * 128]
                    E("act", lambda e, xf=xf, xdst=xdst: e.copy(xdst, xf),
                      reads=[xf], writes=[xdst])
                    E("pe", lambda e, xf=xf, kc=kc, gp=gate_ps: e.matmul(
                        gp, xf, gw_sb[:, kc, :], start=(kc == 0), stop=False,
                        skip_group_check=True),
                      reads=[xf, gw_sb[:, kc, :]], writes=[gate_ps])
                E("pe", lambda e, gp=gate_ps: e.matmul(
                    gp, onesf[0:1, :], gb_sb[0:1, :], start=False, stop=True,
                    skip_group_check=True),
                  reads=[onesf[0:1, :], gb_sb[0:1, :]], writes=[gate_ps])
                E("act", lambda e, gp=gate_ps, o=glog[:, t, :]: e.copy(o, gp),
                  reads=[gate_ps], writes=[glog[:, t, :]])
                vps = pB[0:128, 0:1024] if t % 2 == 0 else pB[0:128, 1024:2048]

                def vgrp(e, t=t, vps=vps):
                    for kc in range(8):
                        lhsT = xnT[:, kc, t * 128:(t + 1) * 128]
                        for nb in range(2):
                            inst = e.matmul(
                                vps[:, nb * 512:(nb + 1) * 512], lhsT,
                                wv_sb[:, kc, nb * 512:(nb + 1) * 512],
                                start=(kc == 0), stop=(kc == 7))
                    return inst

                E("pe", vgrp,
                  reads=[xnT[:, :, t * 128:(t + 1) * 128], wv_sb[:, :, :]],
                  writes=[vps])
                for _h in range(3):
                    E("pe", lambda e: e.matmul(
                        pA[0:128, 1536:2048], trimask[:, 0:128],
                        trimask[:, 0:512], start=True, stop=True,
                        skip_group_check=True),
                      writes=[pA[0:128, 1536:2048]])
                vdst = V_sb[:, t, :, 0:64]
                E("dve", lambda e, vps=vps, vdst=vdst: e.tensor_copy(
                    vdst, vps.rearrange("p (h d) -> p h d", h=16)),
                  reads=[vps], writes=[vdst])

            g3 = glog[:, :, :]
            probs = top2[:, 0, :, :]
            eq = top2[:, 1, :, :]
            c1 = top2[:, 2, :, :]
            c2 = top2[:, 3, :, :]
            f1 = top2[:, 4, :, :]
            w12 = top2[:, 5, :, :]
            mx3 = mxa.rearrange("p (t o) -> p t o", o=1)

            def bc16(ap2d):
                return bass.AP(ap2d.tensor, ap2d.offset,
                               [list(d) for d in ap2d.ap] + [[0, 8]])

            E("dve", lambda e: e.reduce_max(mx3, g3, axis=AX.X, negate=True),
              reads=[g3], writes=[mxa])
            E("dve", lambda e: e.tensor_tensor(probs, g3, bc16(mxa), op=ALU.add),
              reads=[g3, mxa], writes=[probs])
            E("act", lambda e: e.activation(probs, probs, AF.Exp),
              reads=[probs], writes=[probs])
            ssa = stats[:, 96:112]
            ss3 = ssa.rearrange("p (t o) -> p t o", o=1)
            E("dve", lambda e: e.reduce_sum(ss3, probs, axis=AX.X),
              reads=[probs], writes=[ssa])
            E("dve", lambda e: e.reciprocal(ssa, ssa), reads=[ssa], writes=[ssa])
            E("dve", lambda e: e.tensor_tensor(probs, probs, bc16(ssa), op=ALU.mult),
              reads=[probs, ssa], writes=[probs])

            def first_of_max(srcv, fdst):
                E("dve", lambda e: e.reduce_max(mx3, srcv, axis=AX.X),
                  reads=[srcv], writes=[mxa])
                E("dve", lambda e: e.tensor_tensor(eq, srcv, bc16(mxa), op=ALU.is_ge),
                  reads=[srcv, mxa], writes=[eq])
                E("dve", lambda e: e.tensor_copy(c1[:, :, 0:1], eq[:, :, 0:1]),
                  reads=[eq], writes=[c1])
                E("dve", lambda e: e.tensor_tensor(
                    c1[:, :, 1:8], eq[:, :, 1:8], eq[:, :, 0:7], op=ALU.add),
                  reads=[eq], writes=[c1])
                E("dve", lambda e: e.tensor_copy(c2[:, :, 0:2], c1[:, :, 0:2]),
                  reads=[c1], writes=[c2])
                E("dve", lambda e: e.tensor_tensor(
                    c2[:, :, 2:8], c1[:, :, 2:8], c1[:, :, 0:6], op=ALU.add),
                  reads=[c1], writes=[c2])
                E("dve", lambda e: e.tensor_copy(c1[:, :, 0:4], c2[:, :, 0:4]),
                  reads=[c2], writes=[c1])
                E("dve", lambda e: e.tensor_tensor(
                    c1[:, :, 4:8], c2[:, :, 4:8], c2[:, :, 0:4], op=ALU.add),
                  reads=[c2], writes=[c1])
                E("dve", lambda e: e.tensor_scalar(
                    c2, c1, 1.0, None, op0=ALU.is_equal),
                  reads=[c1], writes=[c2])
                E("dve", lambda e, fdst=fdst: e.tensor_tensor(
                    fdst, eq, c2, op=ALU.mult),
                  reads=[eq, c2], writes=[fdst])

            first_of_max(probs, f1)
            E("dve", lambda e: e.tensor_scalar(
                w12, f1, -1.0, 1.0, op0=ALU.mult, op1=ALU.add),
              reads=[f1], writes=[w12])
            E("dve", lambda e: e.tensor_tensor(c1, probs, w12, op=ALU.mult),
              reads=[probs, w12], writes=[c1])
            E("dve", lambda e: e.tensor_copy(w12, c1), reads=[c1], writes=[w12])
            f2 = top2[:, 3, :, :]
            first_of_max(w12, f2)
            E("dve", lambda e: e.tensor_tensor(f1, f1, f2, op=ALU.add),
              reads=[f1, f2], writes=[f1])
            E("dve", lambda e: e.tensor_tensor(cw[:, :, :], probs, f1, op=ALU.mult),
              reads=[probs, f1], writes=[cw[:, :, :]])
            for t in range(NT):
                cwt = cw[:, t, :]
                cwp = pA[0:8, 1536:1664]
                E("pe", lambda e, cwt=cwt, cwp=cwp: e.transpose(
                    cwp, cwt, identf[:, :]),
                  reads=[cwt, identf[:, :]], writes=[cwp])
                cwd = cwT[:, t * 128:(t + 1) * 128]
                E("dve", lambda e, cwp=cwp, cwd=cwd: e.tensor_copy(cwd, cwp),
                  reads=[cwp], writes=[cwd])

            if stop <= 1:
                continue
            p.barrier()

            if stop <= 2:
                continue
            sc = [pA[0:128, 0:1024], pA[0:128, 1024:2048]]
            rbp = pA[0:64, 0:512]
            wop = pB[0:128, 1024:2048]
            pend_norm = [None]

            for hp in range(8):
                if hp + 1 < 8:
                    dma("pool", f"wq{(hp + 1) % 2}", wqb[:, (hp + 1) % 2, :, :], wq_h.ap()[hp + 1, :, :, :])
                    dma("pool", f"wk{(hp + 1) % 2}", wkb[:, (hp + 1) % 2, :, :], wk_h.ap()[hp + 1, :, :, :])
                if hp == 0:
                    dma("pool", "wo", wo_sb[:, :, :],
                        wo_h.ap().rearrange("(c p) d -> p c d", p=128)[:, 0:4, :])

                for half in range(2):
                    ps = sc[half]
                    tk0 = half * 1024

                    def qgrp(e, ps=ps, tk0=tk0, hp=hp):
                        for kc in range(8):
                            lhsT = wqb[:, hp % 2, kc, :]
                            for nb in range(2):
                                a = tk0 + nb * 512
                                inst = e.matmul(
                                    ps[:, nb * 512:(nb + 1) * 512], lhsT,
                                    xnT[:, kc, a:a + 512],
                                    start=(kc == 0), stop=(kc == 7))
                        return inst

                    E("pe", qgrp,
                      reads=[wqb[:, hp % 2, :, :], xnT[:, :, tk0:tk0 + 1024]],
                      writes=[ps])
                    qd = QT[:, tk0:tk0 + 1024]
                    E("dve", lambda e, ps=ps, qd=qd: e.tensor_scalar(
                        qd, ps, BETA, None, op0=ALU.mult),
                      reads=[ps], writes=[qd])

                    def kgrp(e, ps=ps, tk0=tk0, hp=hp):
                        for kc in range(8):
                            lhsT = wkb[:, hp % 2, kc, :]
                            for nb in range(2):
                                a = tk0 + nb * 512
                                inst = e.matmul(
                                    ps[:, nb * 512:(nb + 1) * 512], lhsT,
                                    xnT[:, kc, a:a + 512],
                                    start=(kc == 0), stop=(kc == 7))
                        return inst

                    E("pe", kgrp,
                      reads=[wkb[:, hp % 2, :, :], xnT[:, :, tk0:tk0 + 1024]],
                      writes=[ps])
                    kd = KT[:, tk0:tk0 + 1024]
                    E("dve", lambda e, ps=ps, kd=kd: e.tensor_copy(kd, ps),
                      reads=[ps], writes=[kd])

                h0 = 2 * hp
                for b in range(4):
                    q0 = b * 512
                    nchunks = 4 * (b + 1)
                    av0 = pB[0:65, 0:512]
                    av1 = pB[0:65, 512:1024]
                    pend = None
                    for c in range(nchunks):
                        ps = sc[c % 2]
                        eb = expT[:, c % 3, :]
                        if c == 0 and pend_norm[0] is not None:
                            pend_norm[0]()
                            pend_norm[0] = None

                        def sgrp(e, c=c, ps=ps, q0=q0):
                            e.matmul(ps[:, 0:512],
                                     KT[0:64, c * 128:(c + 1) * 128],
                                     QT[0:64, q0:q0 + 512],
                                     start=True, stop=True,
                                     skip_group_check=True)
                            return e.matmul(ps[:, 512:1024],
                                            KT[64:128, c * 128:(c + 1) * 128],
                                            QT[64:128, q0:q0 + 512],
                                            start=True, stop=True,
                                            skip_group_check=True)

                        E("pe", sgrp,
                          reads=[KT[:, c * 128:(c + 1) * 128], QT[:, q0:q0 + 512]],
                          writes=[ps])
                        E("act", lambda e, ps=ps, eb=eb: e.activation(
                            eb, ps, AF.Exp),
                          reads=[ps], writes=[eb])
                        if c >= 4 * b:
                            koff = (c - 4 * b) * 512
                            mk = trimask[:, koff:koff + 512]
                            E("dve", lambda e, eb=eb, mk=mk: e.tensor_tensor(
                                eb[:, 0:512], eb[:, 0:512], mk, op=ALU.mult),
                              reads=[eb, mk], writes=[eb])
                            E("dve", lambda e, eb=eb, mk=mk: e.tensor_tensor(
                                eb[:, 512:1024], eb[:, 512:1024], mk, op=ALU.mult),
                              reads=[eb, mk], writes=[eb])
                        if pend is not None:
                            pend()
                        for _h in range(2):
                            E("pe", lambda e: e.matmul(
                                pB[0:128, 1024:1536], trimask[:, 0:128],
                                trimask[:, 0:512], start=True, stop=True,
                                skip_group_check=True),
                              writes=[pB[0:128, 1024:1536]])

                        def avgrp(e, c=c, eb=eb, av0=av0, av1=av1, h0=h0,
                                  last=(c == nchunks - 1)):
                            e.matmul(av0, V_sb[:, c, h0, :], eb[:, 0:512],
                                     start=(c == 0), stop=last,
                                     skip_group_check=True)
                            return e.matmul(av1, V_sb[:, c, h0 + 1, :],
                                            eb[:, 512:1024],
                                            start=(c == 0), stop=last,
                                            skip_group_check=True)

                        def mk_pend(c=c, eb=eb, av0=av0, av1=av1, fn=avgrp):
                            def run():
                                E("pe", fn,
                                  reads=[V_sb[:, c, h0, :], V_sb[:, c, h0 + 1, :],
                                         eb],
                                  writes=[av0, av1])
                            return run

                        pend = mk_pend()
                    pend()

                    def mk_norm(av0=av0, av1=av1, hp=hp, q0=q0):
                        def run():
                            for hh in range(2):
                                ap_ = av0 if hh == 0 else av1
                                r0 = hh * 64
                                E("act", lambda e, ap_=ap_: e.activation(
                                    rrow[0:1, :], ap_[64:65, :], AF.Ln),
                                  reads=[ap_], writes=[rrow[0:1, :]])
                                E("act", lambda e: e.activation(
                                    rrow[0:1, :], rrow[0:1, :], AF.Exp,
                                    scale=-1.0),
                                  reads=[rrow[0:1, :]], writes=[rrow[0:1, :]])
                                E("dve", lambda e, ap_=ap_: e.tensor_copy(
                                    avf[:, :], ap_[0:64, :]),
                                  reads=[ap_], writes=[avf[:, :]])
                                E("pe", lambda e: e.matmul(
                                    rbp, onesf[0:1, 0:64], rrow[0:1, :],
                                    start=True, stop=True,
                                    skip_group_check=True),
                                  reads=[onesf[0:1, 0:64], rrow[0:1, :]],
                                  writes=[rbp])
                                avtd = AVT[r0:r0 + 64, hp % 4, q0:q0 + 512]
                                E("dve", lambda e, avtd=avtd: e.tensor_tensor(
                                    avtd, avf[:, :], rbp, op=ALU.mult),
                                  reads=[avf[:, :], rbp], writes=[avtd])
                        return run

                    pend_norm[0] = mk_norm()

                if hp == 3 or hp == 7:
                    if pend_norm[0] is not None:
                        pend_norm[0]()
                        pend_norm[0] = None
                    first = (hp == 3)
                    for t in range(NT):
                        def wogrp(e, t=t, first=first):
                            inst = None
                            for i in range(4):
                                lhsT = AVT[:, i, t * 128:(t + 1) * 128]
                                last_i = (i == 3) and not first
                                for nb in range(2):
                                    inst = e.matmul(
                                        wop[:, nb * 512:(nb + 1) * 512], lhsT,
                                        wo_sb[:, i, nb * 512:(nb + 1) * 512],
                                        start=(i == 0),
                                        stop=(i == 3 and not first),
                                        skip_group_check=True)
                            if first:
                                for nb in range(2):
                                    inst = e.matmul(
                                        wop[:, nb * 512:(nb + 1) * 512],
                                        onesb[0:1, :],
                                        bo_sb[0:1, nb * 512:(nb + 1) * 512],
                                        start=False, stop=True,
                                        skip_group_check=True)
                            return inst

                        E("pe", wogrp,
                          reads=[AVT[:, :, t * 128:(t + 1) * 128],
                                 wo_sb[:, :, :], onesb[0:1, :], bo_sb[0:1, :]],
                          writes=[wop])
                        at = acc[:, t, :]
                        E("dve", lambda e, at=at: e.tensor_tensor(
                            at, at, wop, op=ALU.add),
                          reads=[at, wop], writes=[at])
                    if hp == 3:
                        dma("pool", "wo", wo_sb[:, :, :],
                            wo_h.ap().rearrange("(c p) d -> p c d", p=128)[:, 4:8, :])

            p.barrier()

            if stop <= 3:
                continue
            dma("pool", "w2_0", w2b[:, 0, :, :],
                w2_h.ap()[0, :, :, :].rearrange("c p d -> p c d"))
            dma("pool", "w1_0", w1b[:, 0, :, :, :], w1_h.ap()[0, 0, :, :, :, :])
            for ex in range(9):
                if ex + 1 < 9:
                    dma("pool", f"w2_{(ex + 1) % 2}", w2b[:, (ex + 1) % 2, :, :],
                        w2_h.ap()[ex + 1, :, :, :].rearrange("c p d -> p c d"))
                bcol0 = 128 if ex == 0 else (ex - 1) * 16
                for pr in range(8):
                    nxt = (ex, pr + 1) if pr + 1 < 8 else (ex + 1, 0)
                    if nxt[0] < 9:
                        dma("pool", f"w1_{(pr + 1) % 2}", w1b[:, (pr + 1) % 2, :, :, :],
                            w1_h.ap()[nxt[0], nxt[1], :, :, :, :])

                    def agrp(e, pr=pr, j=0):
                        for kc in range(8):
                            lhsT = w1b[:, pr % 2, kc, 0, :]
                            for nb in range(4):
                                inst = e.matmul(
                                    pA[0:128, nb * 512:(nb + 1) * 512], lhsT,
                                    xnT[:, kc, nb * 512:(nb + 1) * 512],
                                    start=(kc == 0), stop=(kc == 7))
                        return inst

                    E("pe", agrp,
                      reads=[w1b[:, pr % 2, :, 0, :], xnT[:, :, :]],
                      writes=[pA[0:128, 0:2048]])
                    bc_a = biasT[:, bcol0 + pr:bcol0 + pr + 1]
                    E("act", lambda e, bc_a=bc_a: e.activation(
                        sa[:, :], pA[0:128, 0:2048], AF.Silu, bias=bc_a),
                      reads=[pA[0:128, 0:2048], bc_a], writes=[sa[:, :]])

                    def bgrp(e, pr=pr):
                        for kc in range(8):
                            lhsT = w1b[:, pr % 2, kc, 1, :]
                            for nb in range(4):
                                inst = e.matmul(
                                    pB[0:128, nb * 512:(nb + 1) * 512], lhsT,
                                    xnT[:, kc, nb * 512:(nb + 1) * 512],
                                    start=(kc == 0), stop=(kc == 7))
                        return inst

                    E("pe", bgrp,
                      reads=[w1b[:, pr % 2, :, 1, :], xnT[:, :, :]],
                      writes=[pB[0:128, 0:2048]])
                    bc_b = biasT[:, bcol0 + pr + 8:bcol0 + pr + 9]
                    gd = gT[:, pr, :]
                    E("dve", lambda e, bc_b=bc_b, gd=gd: e.scalar_tensor_tensor(
                        gd, pB[0:128, 0:2048], bc_b, sa[:, :],
                        op0=ALU.add, op1=ALU.mult),
                      reads=[pB[0:128, 0:2048], bc_b, sa[:, :]], writes=[gd])

                for t in range(NT):
                    wp = (pA if t % 2 == 0 else pB)[0:128, 0:1024]

                    def w2grp(e, t=t, ex=ex, wp=wp):
                        inst = None
                        for fc in range(8):
                            lhsT = gT[:, fc, t * 128:(t + 1) * 128]
                            for nb in range(2):
                                inst = e.matmul(
                                    wp[:, nb * 512:(nb + 1) * 512], lhsT,
                                    w2b[:, ex % 2, fc, nb * 512:(nb + 1) * 512],
                                    start=(fc == 0),
                                    stop=(fc == 7 and ex != 0),
                                    skip_group_check=True)
                        if ex == 0:
                            for nb in range(2):
                                e.matmul(
                                    wp[:, nb * 512:(nb + 1) * 512],
                                    onesb[0:1, :],
                                    sb2_sb[0:1, nb * 512:(nb + 1) * 512],
                                    start=False, stop=False,
                                    skip_group_check=True)
                                inst = e.matmul(
                                    wp[:, nb * 512:(nb + 1) * 512],
                                    cwT[0:8, t * 128:(t + 1) * 128],
                                    rb2_sb[0:8, nb * 512:(nb + 1) * 512],
                                    start=False, stop=True,
                                    skip_group_check=True)
                        return inst

                    reads = [gT[:, :, t * 128:(t + 1) * 128], w2b[:, ex % 2, :, :]]
                    if ex == 0:
                        reads += [onesb[0:1, :], sb2_sb[0:1, :],
                                  cwT[0:8, t * 128:(t + 1) * 128], rb2_sb[0:8, :]]
                    E("pe", w2grp, reads=reads, writes=[wp])
                    at = acc[:, t, :]
                    if ex == 0:
                        E("dve", lambda e, at=at, wp=wp: e.tensor_tensor(
                            at, at, wp, op=ALU.add),
                          reads=[at, wp], writes=[at])
                    else:
                        cwc = cw[:, t, ex - 1:ex]
                        E("dve", lambda e, at=at, wp=wp, cwc=cwc:
                          e.scalar_tensor_tensor(
                              at, wp, cwc, at, op0=ALU.mult, op1=ALU.add),
                          reads=[at, wp, cwc], writes=[at])
                    if ex == 8:
                        dma("pool", f"st{t}", out[s, t * 128:(t + 1) * 128, :],
                            acc[:, t, :], out_dma=True)

            p.barrier()

        p.final_waits()
        build_program.last_prog = p
        p.install()
    return nc


_PROG_CACHE = {}


def _get_prog(n_seq=NSEQ):
    if n_seq not in _PROG_CACHE:
        _PROG_CACHE[n_seq] = build_program(n_seq)
    return _PROG_CACHE[n_seq]


def _prep_weights(inputs):
    f32 = np.float32
    wq = np.ascontiguousarray(inputs["L0_attn_Wq"], f32)
    wk = np.ascontiguousarray(inputs["L0_attn_Wk"], f32)
    wv = np.ascontiguousarray(inputs["L0_attn_Wv"], f32)
    wo = np.ascontiguousarray(inputs["L0_attn_Wo"], f32)
    bo = np.ascontiguousarray(inputs["L0_attn_bo"], f32).reshape(1, D)
    gw = np.ascontiguousarray(inputs["L0_ffn_gate_w"], f32)
    gb = np.ascontiguousarray(inputs["L0_ffn_gate_b"], f32).reshape(1, 8)
    sw1 = np.ascontiguousarray(inputs["L0_ffn_sw1"], f32)
    sb1 = np.ascontiguousarray(inputs["L0_ffn_sb1"], f32)
    sw2 = np.ascontiguousarray(inputs["L0_ffn_sw2"], f32)
    sb2 = np.ascontiguousarray(inputs["L0_ffn_sb2"], f32)
    rw1 = np.ascontiguousarray(inputs["L0_ffn_rw1"], f32)
    rb1 = np.ascontiguousarray(inputs["L0_ffn_rb1"], f32)
    rw2 = np.ascontiguousarray(inputs["L0_ffn_rw2"], f32)
    rb2 = np.ascontiguousarray(inputs["L0_ffn_rb2"], f32)

    wq_pre = np.ascontiguousarray(
        wq.reshape(8, 128, 8, 128).transpose(2, 1, 0, 3))
    wk_pre = np.ascontiguousarray(
        wk.reshape(8, 128, 8, 128).transpose(2, 1, 0, 3))
    gw_pre = np.ascontiguousarray(gw.reshape(8, 128, 8).transpose(1, 0, 2))
    w1_all = np.concatenate([sw1, rw1], axis=0)
    w1_pre = np.ascontiguousarray(
        w1_all.reshape(9, 8, 128, 2, 8, 128).transpose(0, 4, 2, 1, 3, 5))
    b1 = np.concatenate(
        [rb1.reshape(128, 128), sb1.reshape(16, 128)], axis=0)
    b1 = np.ascontiguousarray(b1)
    w2_all = np.concatenate([sw2, rw2], axis=0).reshape(9, 1024, D)
    w2_pre = np.ascontiguousarray(
        w2_all.reshape(9, 8, 128, D))
    sb2r = np.ascontiguousarray(sb2.reshape(1, D))
    rb2c = np.ascontiguousarray(rb2)
    return dict(wq_pre=wq_pre, wk_pre=wk_pre, wv=wv, wo=wo, bo=bo,
                gw_pre=gw_pre, gb=gb, w1_pre=w1_pre, b1r=b1,
                w2_pre=w2_pre, rb2=rb2c, sb2=sb2r)


def kernel(**inputs):
    x = np.ascontiguousarray(inputs["L0_attn_x"], np.float32)
    w = _prep_weights(inputs)
    nc = _get_prog(NSEQ)
    in_maps = []
    for c in range(N_CORES):
        m = dict(w)
        m["x"] = np.ascontiguousarray(x[c * NSEQ:(c + 1) * NSEQ])
        in_maps.append(m)
    res = run_bass_kernel_spmd(nc, in_maps, core_ids=list(range(N_CORES)))
    outs = [res.results[c]["out"] for c in range(N_CORES)]
    return np.concatenate(outs, axis=0)


# version 10
# speedup vs baseline: 1.0233x; 1.0233x over previous
from contextlib import ExitStack

import numpy as np
import ml_dtypes
import concourse.bass as bass
import concourse.mybir as mybir
from concourse.bass_utils import run_bass_kernel_spmd
import concourse.bass_utils as _bu

if not getattr(_bu, "_ldw_patched", False):
    _orig_run_command = _bu.run_command

    def _run_command_ldw(argv, **kw):
        argv = ["--enable-ldw-opt=true" if a == "--enable-ldw-opt=false" else a
                for a in argv]
        return _orig_run_command(argv, **kw)

    _bu.run_command = _run_command_ldw
    _bu._ldw_patched = True

F32 = mybir.dt.float32
BF16 = mybir.dt.bfloat16
AF = mybir.ActivationFunctionType
ALU = mybir.AluOpType
AX = mybir.AxisListType

N_CORES = 8
SEQ = 2048
D = 1024
NSEQ = 4
NT = SEQ // 128
NH = 16
DH = 64
NE = 8
FF = 1024
EPS = 1e-5
BETA = 0.125


def _prod(xs):
    r = 1
    for v in xs:
        r *= v
    return r


class Prog:

    CENG = ("pe", "act", "dve")

    def __init__(self, nc, ctx):
        self.nc = nc
        self.ctx = ctx
        self.stream = {e: [] for e in ("pe", "act", "dve", "pool", "sync")}
        self.sems = {}
        self.cnt = {}
        self.waited = {e: {} for e in self.stream}
        self.track = {}
        self.engsem = {}
        self.dmasem = {}
        self.psum_names = set()
        self.arena_names = set()
        self.last_barrier = []
        self.out_waits = {}
        self.dma_lanes = {}
        self.dma_lane_sems = set()

    def _mksem(self, name):
        self.sems[name] = self.ctx.enter_context(self.nc.semaphore(name))
        self.cnt[name] = 0
        return name

    def new_tag(self, tag):
        for e in ("pe", "act", "dve", "pool"):
            self.engsem[e] = self._mksem(f"{e}_{tag}")

    def lane(self, key):
        if key not in self.dma_lanes:
            name = self._mksem(f"dl_{key}")
            self.dma_lanes[key] = name
            self.dma_lane_sems.add(name)
        return self.dma_lanes[key]

    def _box(self, ap):
        t = ap.tensor
        name = t.name
        if name in self.psum_names:
            dims = [list(d) for d in ap.ap]
            off = ap.offset
            shape = list(t.shape)
            F = _prod(shape[1:]) if len(shape) > 1 else 1
            rem = off % F if F else 0
            lo = hi = rem
            for s, c in dims[1:]:
                d = s * (c - 1)
                if d >= 0:
                    hi += d
                else:
                    lo += d
            b0 = (lo * 4) // 2048
            b1 = ((hi * 4) // 2048) + 1
            return (name, 0, 1, b0, b1)
        dims = [list(d) for d in ap.ap]
        off = ap.offset
        tname = type(t).__name__
        if "DRam" in tname:
            lo = hi = off
            for s, c in dims:
                d = s * (c - 1)
                if d >= 0:
                    hi += d
                else:
                    lo += d
            return (name, 0, 1, lo, hi + 1)
        shape = list(t.shape)
        F = _prod(shape[1:]) if len(shape) > 1 else 1
        s0, c0 = dims[0]
        ok = F > 0 and s0 != 0 and s0 % F == 0
        if ok:
            for s, c in dims[1:]:
                if c > 1 and abs(s) * (c - 1) >= F:
                    ok = False
                    break
        if not ok:
            return (name, 0, 1 << 20, 0, 1 << 40)
        p0 = off // F
        pstep = s0 // F
        p1 = p0 + pstep * (c0 - 1) + 1
        rem = off % F
        lo = hi = rem
        for s, c in dims[1:]:
            d = s * (c - 1)
            if d >= 0:
                hi += d
            else:
                lo += d
        return (name, p0, p1, lo, hi + 1)

    @staticmethod
    def _ovl(a, b):
        return a[0] == b[0] and not (
            a[2] <= b[1] or b[2] <= a[1] or a[4] <= b[3] or b[4] <= a[3]
        )

    @staticmethod
    def _covers(a, b):
        return (
            a[0] == b[0]
            and a[1] <= b[1]
            and a[2] >= b[2]
            and a[3] <= b[3]
            and a[4] >= b[4]
        )

    def emit(self, eng, fn, reads=(), writes=(), dma=None, out_dma=False):
        waits = {}

        def need(sem, val):
            if sem in self.dma_lane_sems:
                val = self.cnt[sem]
            if val <= 0:
                return
            if self.waited[eng].get(sem, 0) >= val:
                return
            if waits.get(sem, 0) < val:
                waits[sem] = val

        rb = [self._box(ap) for ap in reads]
        wb = [self._box(ap) for ap in writes]
        for b in rb:
            ent = self.track.get(b[0])
            if ent:
                for ob, sem, val in ent["w"]:
                    if self._ovl(b, ob):
                        need(sem, val)
                if b[0] in self.psum_names:
                    for ob, sem, val in ent["r"]:
                        if self._ovl(b, ob):
                            need(sem, val)
        for b in wb:
            ent = self.track.get(b[0])
            if ent:
                for ob, sem, val in ent["w"]:
                    if self._ovl(b, ob):
                        need(sem, val)
                for ob, sem, val in ent["r"]:
                    if self._ovl(b, ob):
                        need(sem, val)
        if dma and any(b[0] in self.arena_names for b in wb):
            for sem, val in self.last_barrier:
                need(sem, val)
        semname = self.lane(dma) if dma else self.engsem[eng]
        amt = 16 if dma else 1
        self.cnt[semname] += amt
        val = self.cnt[semname]
        wl = sorted(waits.items())
        for s, v in wl:
            self.waited[eng][s] = max(self.waited[eng].get(s, 0), v)
        self.stream[eng].append((wl, fn, semname, amt))
        if out_dma:
            self.out_waits[semname] = val
        for b in wb:
            ent = self.track.setdefault(b[0], {"w": [], "r": []})
            ent["w"] = [e_ for e_ in ent["w"] if not self._covers(b, e_[0])]
            ent["r"] = [e_ for e_ in ent["r"] if not self._covers(b, e_[0])]
            ent["w"].append((b, semname, val))
        for b in rb:
            ent = self.track.setdefault(b[0], {"w": [], "r": []})
            ent["r"].append((b, semname, val))
        return val

    def emit_raw(self, eng, fn):
        semname = self.engsem[eng]
        self.cnt[semname] += 1
        self.stream[eng].append(([], fn, semname, 1))

    def barrier(self):
        snap = [(self.engsem[e], self.cnt[self.engsem[e]]) for e in self.CENG]
        snap = [(s, v) for s, v in snap if v > 0]
        for e in self.CENG:
            wl = []
            for s, v in snap:
                if self.waited[e].get(s, 0) < v and s != self.engsem[e]:
                    wl.append((s, v))
                    self.waited[e][s] = v
            if wl:
                self.stream[e].append((wl, None, None, 0))
        self.last_barrier = snap

    def final_waits(self):
        per_eng = {}
        for semname, val in self.out_waits.items():
            per_eng.setdefault("pool", []).append((semname, val))
        for e, wl in per_eng.items():
            wl = [(s, v) for s, v in wl if self.waited[e].get(s, 0) < v]
            if wl:
                self.stream[e].append((sorted(wl), None, None, 0))

    def install(self):
        def runner(eng):
            def _run(e):
                for wl, fn, semname, amt in self.stream[eng]:
                    for s, v in wl:
                        e.wait_ge(self.sems[s], v)
                    if fn is not None:
                        inst = fn(e)
                        inst.then_inc(self.sems[semname], amt)
            return _run

        with self.nc.Block() as block:
            block.tensor(runner("pe"))
            block.scalar(runner("act"))
            block.vector(runner("dve"))
            block.gpsimd(runner("pool"))
            block.sync(runner("sync"))


def build_program(n_seq=NSEQ, stop=99):
    nc = bass.Bass("TRN2", target_bir_lowering=False)

    x_h = nc.dram_tensor("x", [n_seq, SEQ, D], F32, kind="ExternalInput")
    out_h = nc.dram_tensor("out", [n_seq, SEQ, D], F32, kind="ExternalOutput")
    wq_h = nc.dram_tensor("wq_pre", [8, 128, 8, 128], F32, kind="ExternalInput")
    wk_h = nc.dram_tensor("wk_pre", [8, 128, 8, 128], F32, kind="ExternalInput")
    wv_h = nc.dram_tensor("wv", [D, D], F32, kind="ExternalInput")
    wo_h = nc.dram_tensor("wo", [D, D], F32, kind="ExternalInput")
    bo_h = nc.dram_tensor("bo", [1, D], F32, kind="ExternalInput")
    gw_h = nc.dram_tensor("gw_pre", [128, 8, 8], F32, kind="ExternalInput")
    gb_h = nc.dram_tensor("gb", [1, 8], F32, kind="ExternalInput")
    w1_h = nc.dram_tensor("w1_pre", [9, 8, 128, 8, 2, 128], F32, kind="ExternalInput")
    b1_h = nc.dram_tensor("b1r", [144, 128], F32, kind="ExternalInput")
    w2_h = nc.dram_tensor("w2_pre", [9, 8, 128, D], F32, kind="ExternalInput")
    rb2_h = nc.dram_tensor("rb2", [8, D], F32, kind="ExternalInput")
    sb2_h = nc.dram_tensor("sb2", [1, D], F32, kind="ExternalInput")

    ident_np = np.eye(128, dtype=np.float32)
    tri = np.zeros((128, 4, 512), dtype=ml_dtypes.bfloat16)
    for k in range(4):
        for p_ in range(128):
            q0 = k * 128 + p_
            tri[p_, k, q0:] = 1.0
    tri = tri.reshape(128, 2048)
    ones128_f = np.ones((1, 128), dtype=np.float32)
    ones128_b = np.ones((1, 128), dtype=ml_dtypes.bfloat16)
    ident_h = nc.inline_tensor(ident_np, name="identf")
    tri_h = nc.inline_tensor(tri, name="trimask")
    onesf_h = nc.inline_tensor(ones128_f, name="onesf")
    onesb_h = nc.inline_tensor(ones128_b, name="onesb")

    x = x_h.ap()
    out = out_h.ap()

    ctx = ExitStack()
    with ctx:
        RESERVE = 209920
        resv = nc.alloc_sbuf_tensor("manual_reserve", [128, RESERVE], mybir.dt.uint8)
        base = None
        for al in nc.m.functions[0].allocations:
            if isinstance(al, mybir.MemoryLocationSet) and al.memorylocations and                al.memorylocations[0].name == resv.name:
                base = al.memorylocations[0].addr
        assert base is not None
        off = [0]

        def sb(name, shape, dtype, align=64):
            nbytes = _prod(shape[1:]) * (2 if dtype == BF16 else 4)
            o = (off[0] + align - 1) // align * align
            t = nc.alloc_sbuf_tensor_at(name, shape, dtype, offset=base + o)
            off[0] = o + nbytes
            return t

        acc = sb("acc", [128, NT, D], F32)
        xnT = sb("xnT", [128, 8, SEQ], BF16)
        identf = sb("identf_sb", [128, 128], F32)
        trimask = sb("trimask_sb", [128, 2048], BF16)
        onesf = sb("onesf_sb", [1, 128], F32)
        onesb = sb("onesb_sb", [1, 128], BF16)
        biasT = sb("biasT", [128, 144], F32)
        rb2_sb = sb("rb2_sb", [8, D], BF16)
        bo_sb = sb("bo_sb", [1, D], BF16)
        sb2_sb = sb("sb2_sb", [1, D], BF16)
        gw_sb = sb("gw_sb", [128, 8, 8], F32)
        gb_sb = sb("gb_sb", [1, 8], F32)
        cw = sb("cw", [128, NT, 8], F32)
        cwT = sb("cwT", [8, SEQ], BF16)
        stats = sb("stats", [128, 128], F32)
        epsc = sb("epsc", [128, 1], F32)
        rrow = sb("rrow", [1, 512], F32)
        xnf = sb("xnf", [128, 2, 128], F32)
        wqb = sb("wqb", [128, 2, 8, 128], BF16)
        wkb = sb("wkb", [128, 2, 8, 128], BF16)

        arena0 = off[0]
        off[0] = arena0
        V_sb = sb("V_sb", [128, NT, NH, 65], BF16)
        QT = sb("QT", [128, SEQ], BF16)
        KT = sb("KT", [128, SEQ], BF16)
        attn_c = off[0]
        wv_sb = sb("wv_sb", [128, 8, D], BF16)
        xnrow = sb("xnrow", [128, D], F32)
        sq = sb("sq", [128, D], F32)
        b1raw = sb("b1raw", [128, 128], F32)
        b1raw2 = sb("b1raw2", [16, 128], F32)
        glog = sb("glog", [128, NT, 8], F32)
        top2 = sb("top2", [128, 6, NT, 8], F32)
        off[0] = attn_c
        AVT = sb("AVT", [128, 4, SEQ], BF16)
        expT = sb("expT", [128, 3, 1024], BF16)
        avf = sb("avf", [64, 512], F32)
        wo_sb = sb("wo_sb", [128, 4, D], BF16)
        off[0] = arena0
        gT = sb("gT", [128, 8, SEQ], BF16)
        w2b = sb("w2b", [128, 2, 8, D], BF16)
        w1b = sb("w1b", [128, 2, 8, 2, 128], BF16)
        sa = sb("sa", [128, SEQ], F32)
        ffn_end = off[0]
        assert ffn_end <= 209920, f"sbuf overflow {ffn_end}"

        pA = nc.alloc_psum_tensor("pA", [128, 2048], F32)
        pB = nc.alloc_psum_tensor("pB", [128, 2048], F32)

        p = Prog(nc, ctx)
        p.psum_names = {pA.name, pB.name}
        p.arena_names = {t.name for t in [
            V_sb, QT, KT, wv_sb, xnrow, sq, b1raw, b1raw2, glog, top2,
            AVT, expT, avf, wo_sb, gT, w2b, w1b, sa,
        ]}
        p.new_tag("pro")

        E = p.emit

        def dma(eng, lane, dst, src, out_dma=False):
            E(eng, lambda e, d=dst, s=src: e.dma_start(out=d, in_=s),
              reads=[src], writes=[dst], dma=lane, out_dma=out_dma)

        dma("sync", "const", identf[:, :], ident_h.ap())
        dma("sync", "const", trimask[:, :], tri_h.ap())
        dma("sync", "const", onesf[:, :], onesf_h.ap())
        dma("sync", "const", onesb[:, :], onesb_h.ap())
        dma("sync", "const", gw_sb[:, :, :], gw_h.ap())
        dma("sync", "const", gb_sb[:, :], gb_h.ap())
        dma("sync", "const", b1raw[:, :], b1_h.ap()[0:128, :])
        dma("sync", "const", b1raw2[:, :], b1_h.ap()[128:144, :])
        dma("pool", "bias", rb2_sb[:, :], rb2_h.ap())
        dma("pool", "bias", bo_sb[:, :], bo_h.ap())
        dma("pool", "bias", sb2_sb[:, :], sb2_h.ap())

        E("dve", lambda e: e.memset(epsc[:, :], EPS), writes=[epsc[:, :]])
        E("pe", lambda e: e.transpose(pA[0:128, 0:128], b1raw[:, :], identf[:, :]),
          reads=[b1raw[:, :], identf[:, :]], writes=[pA[0:128, 0:128]])
        E("dve", lambda e: e.tensor_copy(biasT[:, 0:128], pA[0:128, 0:128]),
          reads=[pA[0:128, 0:128]], writes=[biasT[:, 0:128]])
        E("pe", lambda e: e.transpose(pA[0:128, 512:528], b1raw2[:, :], identf[0:16, 0:16]),
          reads=[b1raw2[:, :], identf[:, :]], writes=[pA[0:128, 512:528]])
        E("dve", lambda e: e.tensor_copy(biasT[:, 128:144], pA[0:128, 512:528]),
          reads=[pA[0:128, 512:528]], writes=[biasT[:, 128:144]])

        p.barrier()

        for s in range(n_seq):
            for t in range(NT):
                dma("sync", f"xl{t}", acc[:, t, :],
                    x[s, t * 128:(t + 1) * 128, :])
            dma("pool", "wv", wv_sb[:, :, :],
                wv_h.ap().rearrange("(kc p) n -> p kc n", p=128))
            dma("pool", "wq0", wqb[:, 0, :, :], wq_h.ap()[0, :, :, :])
            dma("pool", "wk0", wkb[:, 0, :, :], wk_h.ap()[0, :, :, :])
            E("dve", lambda e: e.memset(V_sb[:, :, :, 64:65], 1.0),
              writes=[V_sb[:, :, :, 64:65]])

            s1a = stats[:, 0:16]
            s2a = stats[:, 16:32]
            ma = stats[:, 32:48]
            va = stats[:, 48:64]
            ra = stats[:, 64:80]
            mxa = stats[:, 80:96]
            for t in range(NT):
                at = acc[:, t, :]
                E("dve", lambda e, at=at, o=s1a[:, t:t + 1]: e.reduce_sum(
                    o, at, axis=AX.X), reads=[at], writes=[s1a[:, t:t + 1]])
                E("act", lambda e, at=at: e.activation(sq[:, :], at, AF.Square),
                  reads=[at], writes=[sq[:, :]])
                E("dve", lambda e, o=s2a[:, t:t + 1]: e.reduce_sum(
                    o, sq[:, :], axis=AX.X), reads=[sq[:, :]],
                  writes=[s2a[:, t:t + 1]])
            E("dve", lambda e: e.tensor_scalar(ma, s1a, 1.0 / D, None, op0=ALU.mult),
              reads=[s1a], writes=[ma])
            E("act", lambda e: e.activation(mxa, ma, AF.Square),
              reads=[ma], writes=[mxa])
            E("dve", lambda e: e.tensor_scalar(va, s2a, 1.0 / D, None, op0=ALU.mult),
              reads=[s2a], writes=[va])
            E("dve", lambda e: e.tensor_tensor(va, va, mxa, op=ALU.subtract),
              reads=[va, mxa], writes=[va])
            E("act", lambda e: e.activation(va, va, AF.Ln, bias=epsc[:, :]),
              reads=[va, epsc[:, :]], writes=[va])
            E("act", lambda e: e.activation(ra, va, AF.Exp, scale=-0.5),
              reads=[va], writes=[ra])
            for t in range(NT):
                at = acc[:, t, :]
                E("dve", lambda e, at=at, m=ma[:, t:t + 1], r=ra[:, t:t + 1]:
                  e.tensor_scalar(xnrow[:, :], at, m, r,
                                  op0=ALU.subtract, op1=ALU.mult),
                  reads=[at, ma[:, t:t + 1], ra[:, t:t + 1]], writes=[xnrow[:, :]])
                gate_ps = pA[0:128, 1024:1032]
                for kc in range(8):
                    xp = pA[0:128, 0:128] if kc % 2 == 0 else pA[0:128, 512:640]
                    xsrc = xnrow[:, kc * 128:(kc + 1) * 128]
                    E("pe", lambda e, xp=xp, xsrc=xsrc: e.transpose(
                        xp, xsrc, identf[:, :]),
                      reads=[xsrc, identf[:, :]], writes=[xp])
                    xf = xnf[:, kc % 2, :]
                    E("dve", lambda e, xp=xp, xf=xf: e.tensor_copy(xf, xp),
                      reads=[xp], writes=[xf])
                    xdst = xnT[:, kc, t * 128:(t + 1) * 128]
                    E("act", lambda e, xf=xf, xdst=xdst: e.copy(xdst, xf),
                      reads=[xf], writes=[xdst])
                    E("pe", lambda e, xf=xf, kc=kc, gp=gate_ps: e.matmul(
                        gp, xf, gw_sb[:, kc, :], start=(kc == 0), stop=False,
                        skip_group_check=True),
                      reads=[xf, gw_sb[:, kc, :]], writes=[gate_ps])
                E("pe", lambda e, gp=gate_ps: e.matmul(
                    gp, onesf[0:1, :], gb_sb[0:1, :], start=False, stop=True,
                    skip_group_check=True),
                  reads=[onesf[0:1, :], gb_sb[0:1, :]], writes=[gate_ps])
                E("act", lambda e, gp=gate_ps, o=glog[:, t, :]: e.copy(o, gp),
                  reads=[gate_ps], writes=[glog[:, t, :]])
                vps = pB[0:128, 0:1024] if t % 2 == 0 else pB[0:128, 1024:2048]

                def vgrp(e, t=t, vps=vps):
                    for kc in range(8):
                        lhsT = xnT[:, kc, t * 128:(t + 1) * 128]
                        for nb in range(2):
                            inst = e.matmul(
                                vps[:, nb * 512:(nb + 1) * 512], lhsT,
                                wv_sb[:, kc, nb * 512:(nb + 1) * 512],
                                start=(kc == 0), stop=(kc == 7))
                    return inst

                E("pe", vgrp,
                  reads=[xnT[:, :, t * 128:(t + 1) * 128], wv_sb[:, :, :]],
                  writes=[vps])
                for _h in range(3):
                    E("pe", lambda e: e.matmul(
                        pA[0:128, 1536:2048], trimask[:, 0:128],
                        trimask[:, 0:512], start=True, stop=True,
                        skip_group_check=True),
                      writes=[pA[0:128, 1536:2048]])
                vdst = V_sb[:, t, :, 0:64]
                E("dve", lambda e, vps=vps, vdst=vdst: e.tensor_copy(
                    vdst, vps.rearrange("p (h d) -> p h d", h=16)),
                  reads=[vps], writes=[vdst])

            g3 = glog[:, :, :]
            probs = top2[:, 0, :, :]
            eq = top2[:, 1, :, :]
            c1 = top2[:, 2, :, :]
            c2 = top2[:, 3, :, :]
            f1 = top2[:, 4, :, :]
            w12 = top2[:, 5, :, :]
            mx3 = mxa.rearrange("p (t o) -> p t o", o=1)

            def bc16(ap2d):
                return bass.AP(ap2d.tensor, ap2d.offset,
                               [list(d) for d in ap2d.ap] + [[0, 8]])

            E("dve", lambda e: e.reduce_max(mx3, g3, axis=AX.X, negate=True),
              reads=[g3], writes=[mxa])
            E("dve", lambda e: e.tensor_tensor(probs, g3, bc16(mxa), op=ALU.add),
              reads=[g3, mxa], writes=[probs])
            E("act", lambda e: e.activation(probs, probs, AF.Exp),
              reads=[probs], writes=[probs])
            ssa = stats[:, 96:112]
            ss3 = ssa.rearrange("p (t o) -> p t o", o=1)
            E("dve", lambda e: e.reduce_sum(ss3, probs, axis=AX.X),
              reads=[probs], writes=[ssa])
            E("dve", lambda e: e.reciprocal(ssa, ssa), reads=[ssa], writes=[ssa])
            E("dve", lambda e: e.tensor_tensor(probs, probs, bc16(ssa), op=ALU.mult),
              reads=[probs, ssa], writes=[probs])

            def first_of_max(srcv, fdst):
                E("dve", lambda e: e.reduce_max(mx3, srcv, axis=AX.X),
                  reads=[srcv], writes=[mxa])
                E("dve", lambda e: e.tensor_tensor(eq, srcv, bc16(mxa), op=ALU.is_ge),
                  reads=[srcv, mxa], writes=[eq])
                E("dve", lambda e: e.tensor_copy(c1[:, :, 0:1], eq[:, :, 0:1]),
                  reads=[eq], writes=[c1])
                E("dve", lambda e: e.tensor_tensor(
                    c1[:, :, 1:8], eq[:, :, 1:8], eq[:, :, 0:7], op=ALU.add),
                  reads=[eq], writes=[c1])
                E("dve", lambda e: e.tensor_copy(c2[:, :, 0:2], c1[:, :, 0:2]),
                  reads=[c1], writes=[c2])
                E("dve", lambda e: e.tensor_tensor(
                    c2[:, :, 2:8], c1[:, :, 2:8], c1[:, :, 0:6], op=ALU.add),
                  reads=[c1], writes=[c2])
                E("dve", lambda e: e.tensor_copy(c1[:, :, 0:4], c2[:, :, 0:4]),
                  reads=[c2], writes=[c1])
                E("dve", lambda e: e.tensor_tensor(
                    c1[:, :, 4:8], c2[:, :, 4:8], c2[:, :, 0:4], op=ALU.add),
                  reads=[c2], writes=[c1])
                E("dve", lambda e: e.tensor_scalar(
                    c2, c1, 1.0, None, op0=ALU.is_equal),
                  reads=[c1], writes=[c2])
                E("dve", lambda e, fdst=fdst: e.tensor_tensor(
                    fdst, eq, c2, op=ALU.mult),
                  reads=[eq, c2], writes=[fdst])

            first_of_max(probs, f1)
            E("dve", lambda e: e.tensor_scalar(
                w12, f1, -1.0, 1.0, op0=ALU.mult, op1=ALU.add),
              reads=[f1], writes=[w12])
            E("dve", lambda e: e.tensor_tensor(c1, probs, w12, op=ALU.mult),
              reads=[probs, w12], writes=[c1])
            E("dve", lambda e: e.tensor_copy(w12, c1), reads=[c1], writes=[w12])
            f2 = top2[:, 3, :, :]
            first_of_max(w12, f2)
            E("dve", lambda e: e.tensor_tensor(f1, f1, f2, op=ALU.add),
              reads=[f1, f2], writes=[f1])
            E("dve", lambda e: e.tensor_tensor(cw[:, :, :], probs, f1, op=ALU.mult),
              reads=[probs, f1], writes=[cw[:, :, :]])
            for t in range(NT):
                cwt = cw[:, t, :]
                cwp = pA[0:8, 1536:1664]
                E("pe", lambda e, cwt=cwt, cwp=cwp: e.transpose(
                    cwp, cwt, identf[:, :]),
                  reads=[cwt, identf[:, :]], writes=[cwp])
                cwd = cwT[:, t * 128:(t + 1) * 128]
                E("dve", lambda e, cwp=cwp, cwd=cwd: e.tensor_copy(cwd, cwp),
                  reads=[cwp], writes=[cwd])

            if stop <= 1:
                continue
            p.barrier()

            if stop <= 2:
                continue
            sc = [pA[0:128, 0:1024], pA[0:128, 1024:2048]]
            rbp = pA[0:64, 0:512]
            wop = pB[0:128, 1024:2048]
            pend_norm = [None]

            for hp in range(8):
                if hp + 1 < 8:
                    dma("pool", f"wq{(hp + 1) % 2}", wqb[:, (hp + 1) % 2, :, :], wq_h.ap()[hp + 1, :, :, :])
                    dma("pool", f"wk{(hp + 1) % 2}", wkb[:, (hp + 1) % 2, :, :], wk_h.ap()[hp + 1, :, :, :])
                if hp == 0:
                    dma("pool", "wo", wo_sb[:, :, :],
                        wo_h.ap().rearrange("(c p) d -> p c d", p=128)[:, 0:4, :])

                for half in range(2):
                    ps = sc[half]
                    tk0 = half * 1024

                    def qgrp(e, ps=ps, tk0=tk0, hp=hp):
                        for kc in range(8):
                            lhsT = wqb[:, hp % 2, kc, :]
                            for nb in range(2):
                                a = tk0 + nb * 512
                                inst = e.matmul(
                                    ps[:, nb * 512:(nb + 1) * 512], lhsT,
                                    xnT[:, kc, a:a + 512],
                                    start=(kc == 0), stop=(kc == 7))
                        return inst

                    E("pe", qgrp,
                      reads=[wqb[:, hp % 2, :, :], xnT[:, :, tk0:tk0 + 1024]],
                      writes=[ps])
                    qd = QT[:, tk0:tk0 + 1024]
                    E("dve", lambda e, ps=ps, qd=qd: e.tensor_scalar(
                        qd, ps, BETA, None, op0=ALU.mult),
                      reads=[ps], writes=[qd])

                    def kgrp(e, ps=ps, tk0=tk0, hp=hp):
                        for kc in range(8):
                            lhsT = wkb[:, hp % 2, kc, :]
                            for nb in range(2):
                                a = tk0 + nb * 512
                                inst = e.matmul(
                                    ps[:, nb * 512:(nb + 1) * 512], lhsT,
                                    xnT[:, kc, a:a + 512],
                                    start=(kc == 0), stop=(kc == 7))
                        return inst

                    E("pe", kgrp,
                      reads=[wkb[:, hp % 2, :, :], xnT[:, :, tk0:tk0 + 1024]],
                      writes=[ps])
                    kd = KT[:, tk0:tk0 + 1024]
                    E("dve", lambda e, ps=ps, kd=kd: e.tensor_copy(kd, ps),
                      reads=[ps], writes=[kd])

                h0 = 2 * hp
                for b in range(4):
                    q0 = b * 512
                    nchunks = 4 * (b + 1)
                    av0 = pB[0:65, 0:512]
                    av1 = pB[0:65, 512:1024]
                    pend = None
                    for c in range(nchunks):
                        ps = sc[c % 2]
                        eb = expT[:, c % 3, :]
                        if c == 0 and pend_norm[0] is not None:
                            pend_norm[0]()
                            pend_norm[0] = None

                        def sgrp(e, c=c, ps=ps, q0=q0):
                            e.matmul(ps[:, 0:512],
                                     KT[0:64, c * 128:(c + 1) * 128],
                                     QT[0:64, q0:q0 + 512],
                                     start=True, stop=True,
                                     skip_group_check=True)
                            return e.matmul(ps[:, 512:1024],
                                            KT[64:128, c * 128:(c + 1) * 128],
                                            QT[64:128, q0:q0 + 512],
                                            start=True, stop=True,
                                            skip_group_check=True)

                        E("pe", sgrp,
                          reads=[KT[:, c * 128:(c + 1) * 128], QT[:, q0:q0 + 512]],
                          writes=[ps])
                        E("act", lambda e, ps=ps, eb=eb: e.activation(
                            eb, ps, AF.Exp),
                          reads=[ps], writes=[eb])
                        if c >= 4 * b:
                            koff = (c - 4 * b) * 512
                            mk = trimask[:, koff:koff + 512]
                            E("dve", lambda e, eb=eb, mk=mk: e.tensor_tensor(
                                eb[:, 0:512], eb[:, 0:512], mk, op=ALU.mult),
                              reads=[eb, mk], writes=[eb])
                            E("dve", lambda e, eb=eb, mk=mk: e.tensor_tensor(
                                eb[:, 512:1024], eb[:, 512:1024], mk, op=ALU.mult),
                              reads=[eb, mk], writes=[eb])
                        if pend is not None:
                            pend()
                        for _h in range(0):
                            E("pe", lambda e: e.matmul(
                                pB[0:128, 1024:1536], trimask[:, 0:128],
                                trimask[:, 0:512], start=True, stop=True,
                                skip_group_check=True),
                              writes=[pB[0:128, 1024:1536]])

                        def avgrp(e, c=c, eb=eb, av0=av0, av1=av1, h0=h0,
                                  last=(c == nchunks - 1)):
                            e.matmul(av0, V_sb[:, c, h0, :], eb[:, 0:512],
                                     start=(c == 0), stop=last,
                                     skip_group_check=True)
                            return e.matmul(av1, V_sb[:, c, h0 + 1, :],
                                            eb[:, 512:1024],
                                            start=(c == 0), stop=last,
                                            skip_group_check=True)

                        def mk_pend(c=c, eb=eb, av0=av0, av1=av1, fn=avgrp):
                            def run():
                                E("pe", fn,
                                  reads=[V_sb[:, c, h0, :], V_sb[:, c, h0 + 1, :],
                                         eb],
                                  writes=[av0, av1])
                            return run

                        pend = mk_pend()
                    pend()

                    def mk_norm(av0=av0, av1=av1, hp=hp, q0=q0):
                        def run():
                            for hh in range(2):
                                ap_ = av0 if hh == 0 else av1
                                r0 = hh * 64
                                E("act", lambda e, ap_=ap_: e.activation(
                                    rrow[0:1, :], ap_[64:65, :], AF.Ln),
                                  reads=[ap_], writes=[rrow[0:1, :]])
                                E("act", lambda e: e.activation(
                                    rrow[0:1, :], rrow[0:1, :], AF.Exp,
                                    scale=-1.0),
                                  reads=[rrow[0:1, :]], writes=[rrow[0:1, :]])
                                E("dve", lambda e, ap_=ap_: e.tensor_copy(
                                    avf[:, :], ap_[0:64, :]),
                                  reads=[ap_], writes=[avf[:, :]])
                                E("pe", lambda e: e.matmul(
                                    rbp, onesf[0:1, 0:64], rrow[0:1, :],
                                    start=True, stop=True,
                                    skip_group_check=True),
                                  reads=[onesf[0:1, 0:64], rrow[0:1, :]],
                                  writes=[rbp])
                                avtd = AVT[r0:r0 + 64, hp % 4, q0:q0 + 512]
                                E("dve", lambda e, avtd=avtd: e.tensor_tensor(
                                    avtd, avf[:, :], rbp, op=ALU.mult),
                                  reads=[avf[:, :], rbp], writes=[avtd])
                        return run

                    pend_norm[0] = mk_norm()

                if hp == 3 or hp == 7:
                    if pend_norm[0] is not None:
                        pend_norm[0]()
                        pend_norm[0] = None
                    first = (hp == 3)
                    for t in range(NT):
                        def wogrp(e, t=t, first=first):
                            inst = None
                            for i in range(4):
                                lhsT = AVT[:, i, t * 128:(t + 1) * 128]
                                last_i = (i == 3) and not first
                                for nb in range(2):
                                    inst = e.matmul(
                                        wop[:, nb * 512:(nb + 1) * 512], lhsT,
                                        wo_sb[:, i, nb * 512:(nb + 1) * 512],
                                        start=(i == 0),
                                        stop=(i == 3 and not first),
                                        skip_group_check=True)
                            if first:
                                for nb in range(2):
                                    inst = e.matmul(
                                        wop[:, nb * 512:(nb + 1) * 512],
                                        onesb[0:1, :],
                                        bo_sb[0:1, nb * 512:(nb + 1) * 512],
                                        start=False, stop=True,
                                        skip_group_check=True)
                            return inst

                        E("pe", wogrp,
                          reads=[AVT[:, :, t * 128:(t + 1) * 128],
                                 wo_sb[:, :, :], onesb[0:1, :], bo_sb[0:1, :]],
                          writes=[wop])
                        at = acc[:, t, :]
                        E("dve", lambda e, at=at: e.tensor_tensor(
                            at, at, wop, op=ALU.add),
                          reads=[at, wop], writes=[at])
                    if hp == 3:
                        dma("pool", "wo", wo_sb[:, :, :],
                            wo_h.ap().rearrange("(c p) d -> p c d", p=128)[:, 4:8, :])

            p.barrier()

            if stop <= 3:
                continue
            dma("pool", "w2_0", w2b[:, 0, :, :],
                w2_h.ap()[0, :, :, :].rearrange("c p d -> p c d"))
            dma("pool", "w1_0", w1b[:, 0, :, :, :], w1_h.ap()[0, 0, :, :, :, :])
            for ex in range(9):
                if ex + 1 < 9:
                    dma("pool", f"w2_{(ex + 1) % 2}", w2b[:, (ex + 1) % 2, :, :],
                        w2_h.ap()[ex + 1, :, :, :].rearrange("c p d -> p c d"))
                bcol0 = 128 if ex == 0 else (ex - 1) * 16
                for pr in range(8):
                    nxt = (ex, pr + 1) if pr + 1 < 8 else (ex + 1, 0)
                    if nxt[0] < 9:
                        dma("pool", f"w1_{(pr + 1) % 2}", w1b[:, (pr + 1) % 2, :, :, :],
                            w1_h.ap()[nxt[0], nxt[1], :, :, :, :])

                    def agrp(e, pr=pr, j=0):
                        for kc in range(8):
                            lhsT = w1b[:, pr % 2, kc, 0, :]
                            for nb in range(4):
                                inst = e.matmul(
                                    pA[0:128, nb * 512:(nb + 1) * 512], lhsT,
                                    xnT[:, kc, nb * 512:(nb + 1) * 512],
                                    start=(kc == 0), stop=(kc == 7))
                        return inst

                    E("pe", agrp,
                      reads=[w1b[:, pr % 2, :, 0, :], xnT[:, :, :]],
                      writes=[pA[0:128, 0:2048]])
                    bc_a = biasT[:, bcol0 + pr:bcol0 + pr + 1]
                    E("act", lambda e, bc_a=bc_a: e.activation(
                        sa[:, :], pA[0:128, 0:2048], AF.Silu, bias=bc_a),
                      reads=[pA[0:128, 0:2048], bc_a], writes=[sa[:, :]])

                    def bgrp(e, pr=pr):
                        for kc in range(8):
                            lhsT = w1b[:, pr % 2, kc, 1, :]
                            for nb in range(4):
                                inst = e.matmul(
                                    pB[0:128, nb * 512:(nb + 1) * 512], lhsT,
                                    xnT[:, kc, nb * 512:(nb + 1) * 512],
                                    start=(kc == 0), stop=(kc == 7))
                        return inst

                    E("pe", bgrp,
                      reads=[w1b[:, pr % 2, :, 1, :], xnT[:, :, :]],
                      writes=[pB[0:128, 0:2048]])
                    bc_b = biasT[:, bcol0 + pr + 8:bcol0 + pr + 9]
                    gd = gT[:, pr, :]
                    E("dve", lambda e, bc_b=bc_b, gd=gd: e.scalar_tensor_tensor(
                        gd, pB[0:128, 0:2048], bc_b, sa[:, :],
                        op0=ALU.add, op1=ALU.mult),
                      reads=[pB[0:128, 0:2048], bc_b, sa[:, :]], writes=[gd])

                for t in range(NT):
                    wp = (pA if t % 2 == 0 else pB)[0:128, 0:1024]

                    def w2grp(e, t=t, ex=ex, wp=wp):
                        inst = None
                        for fc in range(8):
                            lhsT = gT[:, fc, t * 128:(t + 1) * 128]
                            for nb in range(2):
                                inst = e.matmul(
                                    wp[:, nb * 512:(nb + 1) * 512], lhsT,
                                    w2b[:, ex % 2, fc, nb * 512:(nb + 1) * 512],
                                    start=(fc == 0),
                                    stop=(fc == 7 and ex != 0),
                                    skip_group_check=True)
                        if ex == 0:
                            for nb in range(2):
                                e.matmul(
                                    wp[:, nb * 512:(nb + 1) * 512],
                                    onesb[0:1, :],
                                    sb2_sb[0:1, nb * 512:(nb + 1) * 512],
                                    start=False, stop=False,
                                    skip_group_check=True)
                                inst = e.matmul(
                                    wp[:, nb * 512:(nb + 1) * 512],
                                    cwT[0:8, t * 128:(t + 1) * 128],
                                    rb2_sb[0:8, nb * 512:(nb + 1) * 512],
                                    start=False, stop=True,
                                    skip_group_check=True)
                        return inst

                    reads = [gT[:, :, t * 128:(t + 1) * 128], w2b[:, ex % 2, :, :]]
                    if ex == 0:
                        reads += [onesb[0:1, :], sb2_sb[0:1, :],
                                  cwT[0:8, t * 128:(t + 1) * 128], rb2_sb[0:8, :]]
                    E("pe", w2grp, reads=reads, writes=[wp])
                    at = acc[:, t, :]
                    if ex == 0:
                        E("dve", lambda e, at=at, wp=wp: e.tensor_tensor(
                            at, at, wp, op=ALU.add),
                          reads=[at, wp], writes=[at])
                    else:
                        cwc = cw[:, t, ex - 1:ex]
                        E("dve", lambda e, at=at, wp=wp, cwc=cwc:
                          e.scalar_tensor_tensor(
                              at, wp, cwc, at, op0=ALU.mult, op1=ALU.add),
                          reads=[at, wp, cwc], writes=[at])
                    if ex == 8:
                        dma("pool", f"st{t}", out[s, t * 128:(t + 1) * 128, :],
                            acc[:, t, :], out_dma=True)

            p.barrier()

        p.final_waits()
        build_program.last_prog = p
        p.install()
    return nc


_PROG_CACHE = {}


def _get_prog(n_seq=NSEQ):
    if n_seq not in _PROG_CACHE:
        _PROG_CACHE[n_seq] = build_program(n_seq)
    return _PROG_CACHE[n_seq]


def _prep_weights(inputs):
    f32 = np.float32
    wq = np.ascontiguousarray(inputs["L0_attn_Wq"], f32)
    wk = np.ascontiguousarray(inputs["L0_attn_Wk"], f32)
    wv = np.ascontiguousarray(inputs["L0_attn_Wv"], f32)
    wo = np.ascontiguousarray(inputs["L0_attn_Wo"], f32)
    bo = np.ascontiguousarray(inputs["L0_attn_bo"], f32).reshape(1, D)
    gw = np.ascontiguousarray(inputs["L0_ffn_gate_w"], f32)
    gb = np.ascontiguousarray(inputs["L0_ffn_gate_b"], f32).reshape(1, 8)
    sw1 = np.ascontiguousarray(inputs["L0_ffn_sw1"], f32)
    sb1 = np.ascontiguousarray(inputs["L0_ffn_sb1"], f32)
    sw2 = np.ascontiguousarray(inputs["L0_ffn_sw2"], f32)
    sb2 = np.ascontiguousarray(inputs["L0_ffn_sb2"], f32)
    rw1 = np.ascontiguousarray(inputs["L0_ffn_rw1"], f32)
    rb1 = np.ascontiguousarray(inputs["L0_ffn_rb1"], f32)
    rw2 = np.ascontiguousarray(inputs["L0_ffn_rw2"], f32)
    rb2 = np.ascontiguousarray(inputs["L0_ffn_rb2"], f32)

    wq_pre = np.ascontiguousarray(
        wq.reshape(8, 128, 8, 128).transpose(2, 1, 0, 3))
    wk_pre = np.ascontiguousarray(
        wk.reshape(8, 128, 8, 128).transpose(2, 1, 0, 3))
    gw_pre = np.ascontiguousarray(gw.reshape(8, 128, 8).transpose(1, 0, 2))
    w1_all = np.concatenate([sw1, rw1], axis=0)
    w1_pre = np.ascontiguousarray(
        w1_all.reshape(9, 8, 128, 2, 8, 128).transpose(0, 4, 2, 1, 3, 5))
    b1 = np.concatenate(
        [rb1.reshape(128, 128), sb1.reshape(16, 128)], axis=0)
    b1 = np.ascontiguousarray(b1)
    w2_all = np.concatenate([sw2, rw2], axis=0).reshape(9, 1024, D)
    w2_pre = np.ascontiguousarray(
        w2_all.reshape(9, 8, 128, D))
    sb2r = np.ascontiguousarray(sb2.reshape(1, D))
    rb2c = np.ascontiguousarray(rb2)
    return dict(wq_pre=wq_pre, wk_pre=wk_pre, wv=wv, wo=wo, bo=bo,
                gw_pre=gw_pre, gb=gb, w1_pre=w1_pre, b1r=b1,
                w2_pre=w2_pre, rb2=rb2c, sb2=sb2r)


def kernel(**inputs):
    x = np.ascontiguousarray(inputs["L0_attn_x"], np.float32)
    w = _prep_weights(inputs)
    nc = _get_prog(NSEQ)
    in_maps = []
    for c in range(N_CORES):
        m = dict(w)
        m["x"] = np.ascontiguousarray(x[c * NSEQ:(c + 1) * NSEQ])
        in_maps.append(m)
    res = run_bass_kernel_spmd(nc, in_maps, core_ids=list(range(N_CORES)))
    outs = [res.results[c]["out"] for c in range(N_CORES)]
    return np.concatenate(outs, axis=0)


# version 11 (final)
# speedup vs baseline: 1.0705x; 1.0461x over previous
from contextlib import ExitStack

import numpy as np
import ml_dtypes
import concourse.bass as bass
import concourse.mybir as mybir
from concourse.bass_utils import run_bass_kernel_spmd
import concourse.bass_utils as _bu

if not getattr(_bu, "_ldw_patched", False):
    _orig_run_command = _bu.run_command

    def _run_command_ldw(argv, **kw):
        argv = ["--enable-ldw-opt=true" if a == "--enable-ldw-opt=false" else a
                for a in argv]
        return _orig_run_command(argv, **kw)

    _bu.run_command = _run_command_ldw
    _bu._ldw_patched = True

F32 = mybir.dt.float32
BF16 = mybir.dt.bfloat16
AF = mybir.ActivationFunctionType
ALU = mybir.AluOpType
AX = mybir.AxisListType

N_CORES = 8
SEQ = 2048
D = 1024
NSEQ = 4
NT = SEQ // 128
NH = 16
DH = 64
NE = 8
FF = 1024
EPS = 1e-5
BETA = 0.125


def _prod(xs):
    r = 1
    for v in xs:
        r *= v
    return r


class Prog:

    CENG = ("pe", "act", "dve")

    def __init__(self, nc, ctx):
        self.nc = nc
        self.ctx = ctx
        self.stream = {e: [] for e in ("pe", "act", "dve", "pool", "sync")}
        self.sems = {}
        self.cnt = {}
        self.waited = {e: {} for e in self.stream}
        self.track = {}
        self.engsem = {}
        self.dmasem = {}
        self.psum_names = set()
        self.arena_names = set()
        self.last_barrier = []
        self.out_waits = {}
        self.dma_lanes = {}
        self.dma_lane_sems = set()

    def _mksem(self, name):
        self.sems[name] = self.ctx.enter_context(self.nc.semaphore(name))
        self.cnt[name] = 0
        return name

    def new_tag(self, tag):
        for e in ("pe", "act", "dve", "pool"):
            self.engsem[e] = self._mksem(f"{e}_{tag}")

    def lane(self, key):
        if key not in self.dma_lanes:
            name = self._mksem(f"dl_{key}")
            self.dma_lanes[key] = name
            self.dma_lane_sems.add(name)
        return self.dma_lanes[key]

    def _box(self, ap):
        t = ap.tensor
        name = t.name
        if name in self.psum_names:
            dims = [list(d) for d in ap.ap]
            off = ap.offset
            shape = list(t.shape)
            F = _prod(shape[1:]) if len(shape) > 1 else 1
            rem = off % F if F else 0
            lo = hi = rem
            for s, c in dims[1:]:
                d = s * (c - 1)
                if d >= 0:
                    hi += d
                else:
                    lo += d
            b0 = (lo * 4) // 2048
            b1 = ((hi * 4) // 2048) + 1
            return (name, 0, 1, b0, b1)
        dims = [list(d) for d in ap.ap]
        off = ap.offset
        tname = type(t).__name__
        if "DRam" in tname:
            lo = hi = off
            for s, c in dims:
                d = s * (c - 1)
                if d >= 0:
                    hi += d
                else:
                    lo += d
            return (name, 0, 1, lo, hi + 1)
        shape = list(t.shape)
        F = _prod(shape[1:]) if len(shape) > 1 else 1
        s0, c0 = dims[0]
        ok = F > 0 and s0 != 0 and s0 % F == 0
        if ok:
            for s, c in dims[1:]:
                if c > 1 and abs(s) * (c - 1) >= F:
                    ok = False
                    break
        if not ok:
            return (name, 0, 1 << 20, 0, 1 << 40)
        p0 = off // F
        pstep = s0 // F
        p1 = p0 + pstep * (c0 - 1) + 1
        rem = off % F
        lo = hi = rem
        for s, c in dims[1:]:
            d = s * (c - 1)
            if d >= 0:
                hi += d
            else:
                lo += d
        return (name, p0, p1, lo, hi + 1)

    @staticmethod
    def _ovl(a, b):
        return a[0] == b[0] and not (
            a[2] <= b[1] or b[2] <= a[1] or a[4] <= b[3] or b[4] <= a[3]
        )

    @staticmethod
    def _covers(a, b):
        return (
            a[0] == b[0]
            and a[1] <= b[1]
            and a[2] >= b[2]
            and a[3] <= b[3]
            and a[4] >= b[4]
        )

    def emit(self, eng, fn, reads=(), writes=(), dma=None, out_dma=False):
        waits = {}

        def need(sem, val):
            if sem in self.dma_lane_sems:
                val = self.cnt[sem]
            if val <= 0:
                return
            if self.waited[eng].get(sem, 0) >= val:
                return
            if waits.get(sem, 0) < val:
                waits[sem] = val

        rb = [self._box(ap) for ap in reads]
        wb = [self._box(ap) for ap in writes]
        for b in rb:
            ent = self.track.get(b[0])
            if ent:
                for ob, sem, val in ent["w"]:
                    if self._ovl(b, ob):
                        need(sem, val)
                if b[0] in self.psum_names:
                    for ob, sem, val in ent["r"]:
                        if self._ovl(b, ob):
                            need(sem, val)
        for b in wb:
            ent = self.track.get(b[0])
            if ent:
                for ob, sem, val in ent["w"]:
                    if self._ovl(b, ob):
                        need(sem, val)
                for ob, sem, val in ent["r"]:
                    if self._ovl(b, ob):
                        need(sem, val)
        if dma and any(b[0] in self.arena_names for b in wb):
            for sem, val in self.last_barrier:
                need(sem, val)
        semname = self.lane(dma) if dma else self.engsem[eng]
        amt = 16 if dma else 1
        self.cnt[semname] += amt
        val = self.cnt[semname]
        wl = sorted(waits.items())
        for s, v in wl:
            self.waited[eng][s] = max(self.waited[eng].get(s, 0), v)
        self.stream[eng].append((wl, fn, semname, amt))
        if out_dma:
            self.out_waits[semname] = val
        for b in wb:
            ent = self.track.setdefault(b[0], {"w": [], "r": []})
            ent["w"] = [e_ for e_ in ent["w"] if not self._covers(b, e_[0])]
            ent["r"] = [e_ for e_ in ent["r"] if not self._covers(b, e_[0])]
            ent["w"].append((b, semname, val))
        for b in rb:
            ent = self.track.setdefault(b[0], {"w": [], "r": []})
            ent["r"].append((b, semname, val))
        return val

    def emit_raw(self, eng, fn):
        semname = self.engsem[eng]
        self.cnt[semname] += 1
        self.stream[eng].append(([], fn, semname, 1))

    def barrier(self):
        snap = [(self.engsem[e], self.cnt[self.engsem[e]]) for e in self.CENG]
        snap = [(s, v) for s, v in snap if v > 0]
        for e in self.CENG:
            wl = []
            for s, v in snap:
                if self.waited[e].get(s, 0) < v and s != self.engsem[e]:
                    wl.append((s, v))
                    self.waited[e][s] = v
            if wl:
                self.stream[e].append((wl, None, None, 0))
        self.last_barrier = snap

    def final_waits(self):
        per_eng = {}
        for semname, val in self.out_waits.items():
            per_eng.setdefault("pool", []).append((semname, val))
        for e, wl in per_eng.items():
            wl = [(s, v) for s, v in wl if self.waited[e].get(s, 0) < v]
            if wl:
                self.stream[e].append((sorted(wl), None, None, 0))

    def install(self):
        def runner(eng):
            def _run(e):
                for wl, fn, semname, amt in self.stream[eng]:
                    for s, v in wl:
                        e.wait_ge(self.sems[s], v)
                    if fn is not None:
                        inst = fn(e)
                        inst.then_inc(self.sems[semname], amt)
            return _run

        with self.nc.Block() as block:
            block.tensor(runner("pe"))
            block.scalar(runner("act"))
            block.vector(runner("dve"))
            block.gpsimd(runner("pool"))
            block.sync(runner("sync"))


def build_program(n_seq=NSEQ, stop=99):
    nc = bass.Bass("TRN2", target_bir_lowering=False)

    x_h = nc.dram_tensor("x", [n_seq, SEQ, D], F32, kind="ExternalInput")
    out_h = nc.dram_tensor("out", [n_seq, SEQ, D], F32, kind="ExternalOutput")
    wq_h = nc.dram_tensor("wq_pre", [8, 128, 8, 128], F32, kind="ExternalInput")
    wk_h = nc.dram_tensor("wk_pre", [8, 128, 8, 128], F32, kind="ExternalInput")
    wv_h = nc.dram_tensor("wv", [D, D], F32, kind="ExternalInput")
    wo_h = nc.dram_tensor("wo", [D, D], F32, kind="ExternalInput")
    bo_h = nc.dram_tensor("bo", [1, D], F32, kind="ExternalInput")
    gw_h = nc.dram_tensor("gw_pre", [128, 8, 8], F32, kind="ExternalInput")
    gb_h = nc.dram_tensor("gb", [1, 8], F32, kind="ExternalInput")
    w1_h = nc.dram_tensor("w1_pre", [9, 8, 128, 8, 2, 128], F32, kind="ExternalInput")
    b1_h = nc.dram_tensor("b1r", [144, 128], F32, kind="ExternalInput")
    w2_h = nc.dram_tensor("w2_pre", [9, 8, 128, D], F32, kind="ExternalInput")
    rb2_h = nc.dram_tensor("rb2", [8, D], F32, kind="ExternalInput")
    sb2_h = nc.dram_tensor("sb2", [1, D], F32, kind="ExternalInput")

    ident_np = np.eye(128, dtype=np.float32)
    tri = np.zeros((128, 4, 512), dtype=ml_dtypes.bfloat16)
    for k in range(4):
        for p_ in range(128):
            q0 = k * 128 + p_
            tri[p_, k, q0:] = 1.0
    tri = tri.reshape(128, 2048)
    ones128_f = np.ones((1, 128), dtype=np.float32)
    ones128_b = np.ones((1, 128), dtype=ml_dtypes.bfloat16)
    ident_h = nc.inline_tensor(ident_np, name="identf")
    tri_h = nc.inline_tensor(tri, name="trimask")
    onesf_h = nc.inline_tensor(ones128_f, name="onesf")
    onesb_h = nc.inline_tensor(ones128_b, name="onesb")

    x = x_h.ap()
    out = out_h.ap()

    ctx = ExitStack()
    with ctx:
        RESERVE = 209920
        resv = nc.alloc_sbuf_tensor("manual_reserve", [128, RESERVE], mybir.dt.uint8)
        base = None
        for al in nc.m.functions[0].allocations:
            if isinstance(al, mybir.MemoryLocationSet) and al.memorylocations and                al.memorylocations[0].name == resv.name:
                base = al.memorylocations[0].addr
        assert base is not None
        off = [0]

        def sb(name, shape, dtype, align=64):
            nbytes = _prod(shape[1:]) * (2 if dtype == BF16 else 4)
            o = (off[0] + align - 1) // align * align
            t = nc.alloc_sbuf_tensor_at(name, shape, dtype, offset=base + o)
            off[0] = o + nbytes
            return t

        acc = sb("acc", [128, NT, D], F32)
        xnT = sb("xnT", [128, 8, SEQ], BF16)
        identf = sb("identf_sb", [128, 128], F32)
        trimask = sb("trimask_sb", [128, 2048], BF16)
        onesf = sb("onesf_sb", [1, 128], F32)
        onesb = sb("onesb_sb", [1, 128], BF16)
        biasT = sb("biasT", [128, 144], F32)
        rb2_sb = sb("rb2_sb", [8, D], BF16)
        bo_sb = sb("bo_sb", [1, D], BF16)
        sb2_sb = sb("sb2_sb", [1, D], BF16)
        gw_sb = sb("gw_sb", [128, 8, 8], F32)
        gb_sb = sb("gb_sb", [1, 8], F32)
        cw = sb("cw", [128, NT, 8], F32)
        cwT = sb("cwT", [8, SEQ], BF16)
        stats = sb("stats", [128, 128], F32)
        epsc = sb("epsc", [128, 1], F32)
        rrow = sb("rrow", [1, 512], F32)
        xnf = sb("xnf", [128, 2, 128], F32)
        wqb = sb("wqb", [128, 2, 8, 128], BF16)
        wkb = sb("wkb", [128, 2, 8, 128], BF16)

        arena0 = off[0]
        off[0] = arena0
        V_sb = sb("V_sb", [128, NT, NH, 65], BF16)
        QT = sb("QT", [128, SEQ], BF16)
        KT = sb("KT", [128, SEQ], BF16)
        attn_c = off[0]
        wv_sb = sb("wv_sb", [128, 8, D], BF16)
        xnrow = sb("xnrow", [128, D], F32)
        sq = sb("sq", [128, D], F32)
        b1raw = sb("b1raw", [128, 128], F32)
        b1raw2 = sb("b1raw2", [16, 128], F32)
        glog = sb("glog", [128, NT, 8], F32)
        top2 = sb("top2", [128, 6, NT, 8], F32)
        off[0] = attn_c
        AVT = sb("AVT", [128, 4, SEQ], BF16)
        expT = sb("expT", [128, 6, 1024], BF16)
        avf = sb("avf", [64, 512], F32)
        wo_sb = sb("wo_sb", [128, 4, D], BF16)
        off[0] = arena0
        gT = sb("gT", [128, 8, SEQ], BF16)
        w2b = sb("w2b", [128, 2, 8, D], BF16)
        w1b = sb("w1b", [128, 2, 8, 2, 128], BF16)
        sa = sb("sa", [128, SEQ], F32)
        ffn_end = off[0]
        assert ffn_end <= 209920, f"sbuf overflow {ffn_end}"

        pA = nc.alloc_psum_tensor("pA", [128, 2048], F32)
        pB = nc.alloc_psum_tensor("pB", [128, 2048], F32)

        p = Prog(nc, ctx)
        p.psum_names = {pA.name, pB.name}
        p.arena_names = {t.name for t in [
            V_sb, QT, KT, wv_sb, xnrow, sq, b1raw, b1raw2, glog, top2,
            AVT, expT, avf, wo_sb, gT, w2b, w1b, sa,
        ]}
        p.new_tag("pro")

        E = p.emit

        def dma(eng, lane, dst, src, out_dma=False):
            E(eng, lambda e, d=dst, s=src: e.dma_start(out=d, in_=s),
              reads=[src], writes=[dst], dma=lane, out_dma=out_dma)

        dma("sync", "const", identf[:, :], ident_h.ap())
        dma("sync", "const", trimask[:, :], tri_h.ap())
        dma("sync", "const", onesf[:, :], onesf_h.ap())
        dma("sync", "const", onesb[:, :], onesb_h.ap())
        dma("sync", "const", gw_sb[:, :, :], gw_h.ap())
        dma("sync", "const", gb_sb[:, :], gb_h.ap())
        dma("sync", "const", b1raw[:, :], b1_h.ap()[0:128, :])
        dma("sync", "const", b1raw2[:, :], b1_h.ap()[128:144, :])
        dma("pool", "bias", rb2_sb[:, :], rb2_h.ap())
        dma("pool", "bias", bo_sb[:, :], bo_h.ap())
        dma("pool", "bias", sb2_sb[:, :], sb2_h.ap())

        E("dve", lambda e: e.memset(epsc[:, :], EPS), writes=[epsc[:, :]])
        E("pe", lambda e: e.transpose(pA[0:128, 0:128], b1raw[:, :], identf[:, :]),
          reads=[b1raw[:, :], identf[:, :]], writes=[pA[0:128, 0:128]])
        E("dve", lambda e: e.tensor_copy(biasT[:, 0:128], pA[0:128, 0:128]),
          reads=[pA[0:128, 0:128]], writes=[biasT[:, 0:128]])
        E("pe", lambda e: e.transpose(pA[0:128, 512:528], b1raw2[:, :], identf[0:16, 0:16]),
          reads=[b1raw2[:, :], identf[:, :]], writes=[pA[0:128, 512:528]])
        E("dve", lambda e: e.tensor_copy(biasT[:, 128:144], pA[0:128, 512:528]),
          reads=[pA[0:128, 512:528]], writes=[biasT[:, 128:144]])

        p.barrier()

        for s in range(n_seq):
            for t in range(NT):
                dma("sync", f"xl{t}", acc[:, t, :],
                    x[s, t * 128:(t + 1) * 128, :])
            dma("pool", "wv", wv_sb[:, :, :],
                wv_h.ap().rearrange("(kc p) n -> p kc n", p=128))
            dma("pool", "wq0", wqb[:, 0, :, :], wq_h.ap()[0, :, :, :])
            dma("pool", "wk0", wkb[:, 0, :, :], wk_h.ap()[0, :, :, :])
            E("dve", lambda e: e.memset(V_sb[:, :, :, 64:65], 1.0),
              writes=[V_sb[:, :, :, 64:65]])

            s1a = stats[:, 0:16]
            s2a = stats[:, 16:32]
            ma = stats[:, 32:48]
            va = stats[:, 48:64]
            ra = stats[:, 64:80]
            mxa = stats[:, 80:96]
            for t in range(NT):
                at = acc[:, t, :]
                E("dve", lambda e, at=at, o=s1a[:, t:t + 1]: e.reduce_sum(
                    o, at, axis=AX.X), reads=[at], writes=[s1a[:, t:t + 1]])
                E("act", lambda e, at=at: e.activation(sq[:, :], at, AF.Square),
                  reads=[at], writes=[sq[:, :]])
                E("dve", lambda e, o=s2a[:, t:t + 1]: e.reduce_sum(
                    o, sq[:, :], axis=AX.X), reads=[sq[:, :]],
                  writes=[s2a[:, t:t + 1]])
            E("dve", lambda e: e.tensor_scalar(ma, s1a, 1.0 / D, None, op0=ALU.mult),
              reads=[s1a], writes=[ma])
            E("act", lambda e: e.activation(mxa, ma, AF.Square),
              reads=[ma], writes=[mxa])
            E("dve", lambda e: e.tensor_scalar(va, s2a, 1.0 / D, None, op0=ALU.mult),
              reads=[s2a], writes=[va])
            E("dve", lambda e: e.tensor_tensor(va, va, mxa, op=ALU.subtract),
              reads=[va, mxa], writes=[va])
            E("act", lambda e: e.activation(va, va, AF.Ln, bias=epsc[:, :]),
              reads=[va, epsc[:, :]], writes=[va])
            E("act", lambda e: e.activation(ra, va, AF.Exp, scale=-0.5),
              reads=[va], writes=[ra])
            for t in range(NT):
                at = acc[:, t, :]
                E("dve", lambda e, at=at, m=ma[:, t:t + 1], r=ra[:, t:t + 1]:
                  e.tensor_scalar(xnrow[:, :], at, m, r,
                                  op0=ALU.subtract, op1=ALU.mult),
                  reads=[at, ma[:, t:t + 1], ra[:, t:t + 1]], writes=[xnrow[:, :]])
                gate_ps = pA[0:128, 1024:1032]
                for kc in range(8):
                    xp = pA[0:128, 0:128] if kc % 2 == 0 else pA[0:128, 512:640]
                    xsrc = xnrow[:, kc * 128:(kc + 1) * 128]
                    E("pe", lambda e, xp=xp, xsrc=xsrc: e.transpose(
                        xp, xsrc, identf[:, :]),
                      reads=[xsrc, identf[:, :]], writes=[xp])
                    xf = xnf[:, kc % 2, :]
                    E("dve", lambda e, xp=xp, xf=xf: e.tensor_copy(xf, xp),
                      reads=[xp], writes=[xf])
                    xdst = xnT[:, kc, t * 128:(t + 1) * 128]
                    E("act", lambda e, xf=xf, xdst=xdst: e.copy(xdst, xf),
                      reads=[xf], writes=[xdst])
                    E("pe", lambda e, xf=xf, kc=kc, gp=gate_ps: e.matmul(
                        gp, xf, gw_sb[:, kc, :], start=(kc == 0), stop=False,
                        skip_group_check=True),
                      reads=[xf, gw_sb[:, kc, :]], writes=[gate_ps])
                E("pe", lambda e, gp=gate_ps: e.matmul(
                    gp, onesf[0:1, :], gb_sb[0:1, :], start=False, stop=True,
                    skip_group_check=True),
                  reads=[onesf[0:1, :], gb_sb[0:1, :]], writes=[gate_ps])
                E("act", lambda e, gp=gate_ps, o=glog[:, t, :]: e.copy(o, gp),
                  reads=[gate_ps], writes=[glog[:, t, :]])
                vps = pB[0:128, 0:1024] if t % 2 == 0 else pB[0:128, 1024:2048]

                def vgrp(e, t=t, vps=vps):
                    for kc in range(8):
                        lhsT = xnT[:, kc, t * 128:(t + 1) * 128]
                        for nb in range(2):
                            inst = e.matmul(
                                vps[:, nb * 512:(nb + 1) * 512], lhsT,
                                wv_sb[:, kc, nb * 512:(nb + 1) * 512],
                                start=(kc == 0), stop=(kc == 7))
                    return inst

                E("pe", vgrp,
                  reads=[xnT[:, :, t * 128:(t + 1) * 128], wv_sb[:, :, :]],
                  writes=[vps])
                for _h in range(3):
                    E("pe", lambda e: e.matmul(
                        pA[0:128, 1536:2048], trimask[:, 0:128],
                        trimask[:, 0:512], start=True, stop=True,
                        skip_group_check=True),
                      writes=[pA[0:128, 1536:2048]])
                vdst = V_sb[:, t, :, 0:64]
                E("dve", lambda e, vps=vps, vdst=vdst: e.tensor_copy(
                    vdst, vps.rearrange("p (h d) -> p h d", h=16)),
                  reads=[vps], writes=[vdst])

            g3 = glog[:, :, :]
            probs = top2[:, 0, :, :]
            eq = top2[:, 1, :, :]
            c1 = top2[:, 2, :, :]
            c2 = top2[:, 3, :, :]
            f1 = top2[:, 4, :, :]
            w12 = top2[:, 5, :, :]
            mx3 = mxa.rearrange("p (t o) -> p t o", o=1)

            def bc16(ap2d):
                return bass.AP(ap2d.tensor, ap2d.offset,
                               [list(d) for d in ap2d.ap] + [[0, 8]])

            E("dve", lambda e: e.reduce_max(mx3, g3, axis=AX.X, negate=True),
              reads=[g3], writes=[mxa])
            E("dve", lambda e: e.tensor_tensor(probs, g3, bc16(mxa), op=ALU.add),
              reads=[g3, mxa], writes=[probs])
            E("act", lambda e: e.activation(probs, probs, AF.Exp),
              reads=[probs], writes=[probs])
            ssa = stats[:, 96:112]
            ss3 = ssa.rearrange("p (t o) -> p t o", o=1)
            E("dve", lambda e: e.reduce_sum(ss3, probs, axis=AX.X),
              reads=[probs], writes=[ssa])
            E("dve", lambda e: e.reciprocal(ssa, ssa), reads=[ssa], writes=[ssa])
            E("dve", lambda e: e.tensor_tensor(probs, probs, bc16(ssa), op=ALU.mult),
              reads=[probs, ssa], writes=[probs])

            def first_of_max(srcv, fdst):
                E("dve", lambda e: e.reduce_max(mx3, srcv, axis=AX.X),
                  reads=[srcv], writes=[mxa])
                E("dve", lambda e: e.tensor_tensor(eq, srcv, bc16(mxa), op=ALU.is_ge),
                  reads=[srcv, mxa], writes=[eq])
                E("dve", lambda e: e.tensor_copy(c1[:, :, 0:1], eq[:, :, 0:1]),
                  reads=[eq], writes=[c1])
                E("dve", lambda e: e.tensor_tensor(
                    c1[:, :, 1:8], eq[:, :, 1:8], eq[:, :, 0:7], op=ALU.add),
                  reads=[eq], writes=[c1])
                E("dve", lambda e: e.tensor_copy(c2[:, :, 0:2], c1[:, :, 0:2]),
                  reads=[c1], writes=[c2])
                E("dve", lambda e: e.tensor_tensor(
                    c2[:, :, 2:8], c1[:, :, 2:8], c1[:, :, 0:6], op=ALU.add),
                  reads=[c1], writes=[c2])
                E("dve", lambda e: e.tensor_copy(c1[:, :, 0:4], c2[:, :, 0:4]),
                  reads=[c2], writes=[c1])
                E("dve", lambda e: e.tensor_tensor(
                    c1[:, :, 4:8], c2[:, :, 4:8], c2[:, :, 0:4], op=ALU.add),
                  reads=[c2], writes=[c1])
                E("dve", lambda e: e.tensor_scalar(
                    c2, c1, 1.0, None, op0=ALU.is_equal),
                  reads=[c1], writes=[c2])
                E("dve", lambda e, fdst=fdst: e.tensor_tensor(
                    fdst, eq, c2, op=ALU.mult),
                  reads=[eq, c2], writes=[fdst])

            first_of_max(probs, f1)
            E("dve", lambda e: e.tensor_scalar(
                w12, f1, -1.0, 1.0, op0=ALU.mult, op1=ALU.add),
              reads=[f1], writes=[w12])
            E("dve", lambda e: e.tensor_tensor(c1, probs, w12, op=ALU.mult),
              reads=[probs, w12], writes=[c1])
            E("dve", lambda e: e.tensor_copy(w12, c1), reads=[c1], writes=[w12])
            f2 = top2[:, 3, :, :]
            first_of_max(w12, f2)
            E("dve", lambda e: e.tensor_tensor(f1, f1, f2, op=ALU.add),
              reads=[f1, f2], writes=[f1])
            E("dve", lambda e: e.tensor_tensor(cw[:, :, :], probs, f1, op=ALU.mult),
              reads=[probs, f1], writes=[cw[:, :, :]])
            for t in range(NT):
                cwt = cw[:, t, :]
                cwp = pA[0:8, 1536:1664]
                E("pe", lambda e, cwt=cwt, cwp=cwp: e.transpose(
                    cwp, cwt, identf[:, :]),
                  reads=[cwt, identf[:, :]], writes=[cwp])
                cwd = cwT[:, t * 128:(t + 1) * 128]
                E("dve", lambda e, cwp=cwp, cwd=cwd: e.tensor_copy(cwd, cwp),
                  reads=[cwp], writes=[cwd])

            if stop <= 1:
                continue
            p.barrier()

            if stop <= 2:
                continue
            sc = [pA[0:128, 0:1024], pA[0:128, 1024:2048], pB[0:128, 1024:2048]]
            rbp = pA[0:64, 0:512]
            wop = pB[0:128, 1024:2048]
            pend_norm = [None]
            cglob = [0]

            for hp in range(8):
                if hp + 1 < 8:
                    dma("pool", f"wq{(hp + 1) % 2}", wqb[:, (hp + 1) % 2, :, :], wq_h.ap()[hp + 1, :, :, :])
                    dma("pool", f"wk{(hp + 1) % 2}", wkb[:, (hp + 1) % 2, :, :], wk_h.ap()[hp + 1, :, :, :])
                if hp == 0:
                    dma("pool", "wo", wo_sb[:, :, :],
                        wo_h.ap().rearrange("(c p) d -> p c d", p=128)[:, 0:4, :])

                for half in range(2):
                    ps = sc[half]
                    tk0 = half * 1024

                    def qgrp(e, ps=ps, tk0=tk0, hp=hp):
                        for kc in range(8):
                            lhsT = wqb[:, hp % 2, kc, :]
                            for nb in range(2):
                                a = tk0 + nb * 512
                                inst = e.matmul(
                                    ps[:, nb * 512:(nb + 1) * 512], lhsT,
                                    xnT[:, kc, a:a + 512],
                                    start=(kc == 0), stop=(kc == 7))
                        return inst

                    E("pe", qgrp,
                      reads=[wqb[:, hp % 2, :, :], xnT[:, :, tk0:tk0 + 1024]],
                      writes=[ps])
                    qd = QT[:, tk0:tk0 + 1024]
                    E("dve", lambda e, ps=ps, qd=qd: e.tensor_scalar(
                        qd, ps, BETA, None, op0=ALU.mult),
                      reads=[ps], writes=[qd])

                    def kgrp(e, ps=ps, tk0=tk0, hp=hp):
                        for kc in range(8):
                            lhsT = wkb[:, hp % 2, kc, :]
                            for nb in range(2):
                                a = tk0 + nb * 512
                                inst = e.matmul(
                                    ps[:, nb * 512:(nb + 1) * 512], lhsT,
                                    xnT[:, kc, a:a + 512],
                                    start=(kc == 0), stop=(kc == 7))
                        return inst

                    E("pe", kgrp,
                      reads=[wkb[:, hp % 2, :, :], xnT[:, :, tk0:tk0 + 1024]],
                      writes=[ps])
                    kd = KT[:, tk0:tk0 + 1024]
                    E("dve", lambda e, ps=ps, kd=kd: e.tensor_copy(kd, ps),
                      reads=[ps], writes=[kd])

                h0 = 2 * hp
                for b in range(4):
                    q0 = b * 512
                    nchunks = 4 * (b + 1)
                    av0 = pB[0:65, 0:512]
                    av1 = pB[0:65, 512:1024]
                    pend_q = []
                    for c in range(nchunks):
                        cg = cglob[0]
                        cglob[0] += 1
                        ps = sc[cg % 3]
                        eb = expT[:, cg % 6, :]
                        if c == 0 and pend_norm[0] is not None:
                            pend_norm[0]()
                            pend_norm[0] = None

                        def sgrp(e, c=c, ps=ps, q0=q0):
                            e.matmul(ps[:, 0:512],
                                     KT[0:64, c * 128:(c + 1) * 128],
                                     QT[0:64, q0:q0 + 512],
                                     start=True, stop=True,
                                     skip_group_check=True)
                            return e.matmul(ps[:, 512:1024],
                                            KT[64:128, c * 128:(c + 1) * 128],
                                            QT[64:128, q0:q0 + 512],
                                            start=True, stop=True,
                                            skip_group_check=True)

                        E("pe", sgrp,
                          reads=[KT[:, c * 128:(c + 1) * 128], QT[:, q0:q0 + 512]],
                          writes=[ps])
                        E("act", lambda e, ps=ps, eb=eb: e.activation(
                            eb, ps, AF.Exp),
                          reads=[ps], writes=[eb])
                        if c >= 4 * b:
                            koff = (c - 4 * b) * 512
                            mk = trimask[:, koff:koff + 512]
                            E("dve", lambda e, eb=eb, mk=mk: e.tensor_tensor(
                                eb[:, 0:512], eb[:, 0:512], mk, op=ALU.mult),
                              reads=[eb, mk], writes=[eb])
                            E("dve", lambda e, eb=eb, mk=mk: e.tensor_tensor(
                                eb[:, 512:1024], eb[:, 512:1024], mk, op=ALU.mult),
                              reads=[eb, mk], writes=[eb])
                        if len(pend_q) >= 2:
                            pend_q.pop(0)()
                        for _h in range(0):
                            E("pe", lambda e: e.matmul(
                                pB[0:128, 1024:1536], trimask[:, 0:128],
                                trimask[:, 0:512], start=True, stop=True,
                                skip_group_check=True),
                              writes=[pB[0:128, 1024:1536]])

                        def avgrp(e, c=c, eb=eb, av0=av0, av1=av1, h0=h0,
                                  last=(c == nchunks - 1)):
                            e.matmul(av0, V_sb[:, c, h0, :], eb[:, 0:512],
                                     start=(c == 0), stop=last,
                                     skip_group_check=True)
                            return e.matmul(av1, V_sb[:, c, h0 + 1, :],
                                            eb[:, 512:1024],
                                            start=(c == 0), stop=last,
                                            skip_group_check=True)

                        def mk_pend(c=c, eb=eb, av0=av0, av1=av1, fn=avgrp):
                            def run():
                                E("pe", fn,
                                  reads=[V_sb[:, c, h0, :], V_sb[:, c, h0 + 1, :],
                                         eb],
                                  writes=[av0, av1])
                            return run

                        pend_q.append(mk_pend())
                    for fn_ in pend_q:
                        fn_()
                    pend_q = []

                    def mk_norm(av0=av0, av1=av1, hp=hp, q0=q0):
                        def run():
                            for hh in range(2):
                                ap_ = av0 if hh == 0 else av1
                                r0 = hh * 64
                                E("act", lambda e, ap_=ap_: e.activation(
                                    rrow[0:1, :], ap_[64:65, :], AF.Ln),
                                  reads=[ap_], writes=[rrow[0:1, :]])
                                E("act", lambda e: e.activation(
                                    rrow[0:1, :], rrow[0:1, :], AF.Exp,
                                    scale=-1.0),
                                  reads=[rrow[0:1, :]], writes=[rrow[0:1, :]])
                                E("dve", lambda e, ap_=ap_: e.tensor_copy(
                                    avf[:, :], ap_[0:64, :]),
                                  reads=[ap_], writes=[avf[:, :]])
                                E("pe", lambda e: e.matmul(
                                    rbp, onesf[0:1, 0:64], rrow[0:1, :],
                                    start=True, stop=True,
                                    skip_group_check=True),
                                  reads=[onesf[0:1, 0:64], rrow[0:1, :]],
                                  writes=[rbp])
                                avtd = AVT[r0:r0 + 64, hp % 4, q0:q0 + 512]
                                E("dve", lambda e, avtd=avtd: e.tensor_tensor(
                                    avtd, avf[:, :], rbp, op=ALU.mult),
                                  reads=[avf[:, :], rbp], writes=[avtd])
                        return run

                    pend_norm[0] = mk_norm()

                if hp == 3 or hp == 7:
                    if pend_norm[0] is not None:
                        pend_norm[0]()
                        pend_norm[0] = None
                    first = (hp == 3)
                    for t in range(NT):
                        def wogrp(e, t=t, first=first):
                            inst = None
                            for i in range(4):
                                lhsT = AVT[:, i, t * 128:(t + 1) * 128]
                                last_i = (i == 3) and not first
                                for nb in range(2):
                                    inst = e.matmul(
                                        wop[:, nb * 512:(nb + 1) * 512], lhsT,
                                        wo_sb[:, i, nb * 512:(nb + 1) * 512],
                                        start=(i == 0),
                                        stop=(i == 3 and not first),
                                        skip_group_check=True)
                            if first:
                                for nb in range(2):
                                    inst = e.matmul(
                                        wop[:, nb * 512:(nb + 1) * 512],
                                        onesb[0:1, :],
                                        bo_sb[0:1, nb * 512:(nb + 1) * 512],
                                        start=False, stop=True,
                                        skip_group_check=True)
                            return inst

                        E("pe", wogrp,
                          reads=[AVT[:, :, t * 128:(t + 1) * 128],
                                 wo_sb[:, :, :], onesb[0:1, :], bo_sb[0:1, :]],
                          writes=[wop])
                        at = acc[:, t, :]
                        E("dve", lambda e, at=at: e.tensor_tensor(
                            at, at, wop, op=ALU.add),
                          reads=[at, wop], writes=[at])
                    if hp == 3:
                        dma("pool", "wo", wo_sb[:, :, :],
                            wo_h.ap().rearrange("(c p) d -> p c d", p=128)[:, 4:8, :])

            p.barrier()

            if stop <= 3:
                continue
            dma("pool", "w2_0", w2b[:, 0, :, :],
                w2_h.ap()[0, :, :, :].rearrange("c p d -> p c d"))
            dma("pool", "w1_0", w1b[:, 0, :, :, :], w1_h.ap()[0, 0, :, :, :, :])
            for ex in range(9):
                if ex + 1 < 9:
                    dma("pool", f"w2_{(ex + 1) % 2}", w2b[:, (ex + 1) % 2, :, :],
                        w2_h.ap()[ex + 1, :, :, :].rearrange("c p d -> p c d"))
                bcol0 = 128 if ex == 0 else (ex - 1) * 16
                for pr in range(8):
                    nxt = (ex, pr + 1) if pr + 1 < 8 else (ex + 1, 0)
                    if nxt[0] < 9:
                        dma("pool", f"w1_{(pr + 1) % 2}", w1b[:, (pr + 1) % 2, :, :, :],
                            w1_h.ap()[nxt[0], nxt[1], :, :, :, :])

                    def agrp(e, pr=pr, j=0):
                        for kc in range(8):
                            lhsT = w1b[:, pr % 2, kc, 0, :]
                            for nb in range(4):
                                inst = e.matmul(
                                    pA[0:128, nb * 512:(nb + 1) * 512], lhsT,
                                    xnT[:, kc, nb * 512:(nb + 1) * 512],
                                    start=(kc == 0), stop=(kc == 7))
                        return inst

                    E("pe", agrp,
                      reads=[w1b[:, pr % 2, :, 0, :], xnT[:, :, :]],
                      writes=[pA[0:128, 0:2048]])
                    bc_a = biasT[:, bcol0 + pr:bcol0 + pr + 1]
                    E("act", lambda e, bc_a=bc_a: e.activation(
                        sa[:, :], pA[0:128, 0:2048], AF.Silu, bias=bc_a),
                      reads=[pA[0:128, 0:2048], bc_a], writes=[sa[:, :]])

                    def bgrp(e, pr=pr):
                        for kc in range(8):
                            lhsT = w1b[:, pr % 2, kc, 1, :]
                            for nb in range(4):
                                inst = e.matmul(
                                    pB[0:128, nb * 512:(nb + 1) * 512], lhsT,
                                    xnT[:, kc, nb * 512:(nb + 1) * 512],
                                    start=(kc == 0), stop=(kc == 7))
                        return inst

                    E("pe", bgrp,
                      reads=[w1b[:, pr % 2, :, 1, :], xnT[:, :, :]],
                      writes=[pB[0:128, 0:2048]])
                    bc_b = biasT[:, bcol0 + pr + 8:bcol0 + pr + 9]
                    gd = gT[:, pr, :]
                    E("dve", lambda e, bc_b=bc_b, gd=gd: e.scalar_tensor_tensor(
                        gd, pB[0:128, 0:2048], bc_b, sa[:, :],
                        op0=ALU.add, op1=ALU.mult),
                      reads=[pB[0:128, 0:2048], bc_b, sa[:, :]], writes=[gd])

                for t in range(NT):
                    wp = (pA if t % 2 == 0 else pB)[0:128, 0:1024]

                    def w2grp(e, t=t, ex=ex, wp=wp):
                        inst = None
                        for fc in range(8):
                            lhsT = gT[:, fc, t * 128:(t + 1) * 128]
                            for nb in range(2):
                                inst = e.matmul(
                                    wp[:, nb * 512:(nb + 1) * 512], lhsT,
                                    w2b[:, ex % 2, fc, nb * 512:(nb + 1) * 512],
                                    start=(fc == 0),
                                    stop=(fc == 7 and ex != 0),
                                    skip_group_check=True)
                        if ex == 0:
                            for nb in range(2):
                                e.matmul(
                                    wp[:, nb * 512:(nb + 1) * 512],
                                    onesb[0:1, :],
                                    sb2_sb[0:1, nb * 512:(nb + 1) * 512],
                                    start=False, stop=False,
                                    skip_group_check=True)
                                inst = e.matmul(
                                    wp[:, nb * 512:(nb + 1) * 512],
                                    cwT[0:8, t * 128:(t + 1) * 128],
                                    rb2_sb[0:8, nb * 512:(nb + 1) * 512],
                                    start=False, stop=True,
                                    skip_group_check=True)
                        return inst

                    reads = [gT[:, :, t * 128:(t + 1) * 128], w2b[:, ex % 2, :, :]]
                    if ex == 0:
                        reads += [onesb[0:1, :], sb2_sb[0:1, :],
                                  cwT[0:8, t * 128:(t + 1) * 128], rb2_sb[0:8, :]]
                    E("pe", w2grp, reads=reads, writes=[wp])
                    at = acc[:, t, :]
                    if ex == 0:
                        E("dve", lambda e, at=at, wp=wp: e.tensor_tensor(
                            at, at, wp, op=ALU.add),
                          reads=[at, wp], writes=[at])
                    else:
                        cwc = cw[:, t, ex - 1:ex]
                        E("dve", lambda e, at=at, wp=wp, cwc=cwc:
                          e.scalar_tensor_tensor(
                              at, wp, cwc, at, op0=ALU.mult, op1=ALU.add),
                          reads=[at, wp, cwc], writes=[at])
                    if ex == 8:
                        dma("pool", f"st{t}", out[s, t * 128:(t + 1) * 128, :],
                            acc[:, t, :], out_dma=True)

            p.barrier()

        p.final_waits()
        build_program.last_prog = p
        p.install()
    return nc


_PROG_CACHE = {}


def _get_prog(n_seq=NSEQ):
    if n_seq not in _PROG_CACHE:
        _PROG_CACHE[n_seq] = build_program(n_seq)
    return _PROG_CACHE[n_seq]


def _prep_weights(inputs):
    f32 = np.float32
    wq = np.ascontiguousarray(inputs["L0_attn_Wq"], f32)
    wk = np.ascontiguousarray(inputs["L0_attn_Wk"], f32)
    wv = np.ascontiguousarray(inputs["L0_attn_Wv"], f32)
    wo = np.ascontiguousarray(inputs["L0_attn_Wo"], f32)
    bo = np.ascontiguousarray(inputs["L0_attn_bo"], f32).reshape(1, D)
    gw = np.ascontiguousarray(inputs["L0_ffn_gate_w"], f32)
    gb = np.ascontiguousarray(inputs["L0_ffn_gate_b"], f32).reshape(1, 8)
    sw1 = np.ascontiguousarray(inputs["L0_ffn_sw1"], f32)
    sb1 = np.ascontiguousarray(inputs["L0_ffn_sb1"], f32)
    sw2 = np.ascontiguousarray(inputs["L0_ffn_sw2"], f32)
    sb2 = np.ascontiguousarray(inputs["L0_ffn_sb2"], f32)
    rw1 = np.ascontiguousarray(inputs["L0_ffn_rw1"], f32)
    rb1 = np.ascontiguousarray(inputs["L0_ffn_rb1"], f32)
    rw2 = np.ascontiguousarray(inputs["L0_ffn_rw2"], f32)
    rb2 = np.ascontiguousarray(inputs["L0_ffn_rb2"], f32)

    wq_pre = np.ascontiguousarray(
        wq.reshape(8, 128, 8, 128).transpose(2, 1, 0, 3))
    wk_pre = np.ascontiguousarray(
        wk.reshape(8, 128, 8, 128).transpose(2, 1, 0, 3))
    gw_pre = np.ascontiguousarray(gw.reshape(8, 128, 8).transpose(1, 0, 2))
    w1_all = np.concatenate([sw1, rw1], axis=0)
    w1_pre = np.ascontiguousarray(
        w1_all.reshape(9, 8, 128, 2, 8, 128).transpose(0, 4, 2, 1, 3, 5))
    b1 = np.concatenate(
        [rb1.reshape(128, 128), sb1.reshape(16, 128)], axis=0)
    b1 = np.ascontiguousarray(b1)
    w2_all = np.concatenate([sw2, rw2], axis=0).reshape(9, 1024, D)
    w2_pre = np.ascontiguousarray(
        w2_all.reshape(9, 8, 128, D))
    sb2r = np.ascontiguousarray(sb2.reshape(1, D))
    rb2c = np.ascontiguousarray(rb2)
    return dict(wq_pre=wq_pre, wk_pre=wk_pre, wv=wv, wo=wo, bo=bo,
                gw_pre=gw_pre, gb=gb, w1_pre=w1_pre, b1r=b1,
                w2_pre=w2_pre, rb2=rb2c, sb2=sb2r)


def kernel(**inputs):
    x = np.ascontiguousarray(inputs["L0_attn_x"], np.float32)
    w = _prep_weights(inputs)
    nc = _get_prog(NSEQ)
    in_maps = []
    for c in range(N_CORES):
        m = dict(w)
        m["x"] = np.ascontiguousarray(x[c * NSEQ:(c + 1) * NSEQ])
        in_maps.append(m)
    res = run_bass_kernel_spmd(nc, in_maps, core_ids=list(range(N_CORES)))
    outs = [res.results[c]["out"] for c in range(N_CORES)]
    return np.concatenate(outs, axis=0)


# version 12
# speedup vs baseline: 1.2692x; 1.1856x over previous
from contextlib import ExitStack

import numpy as np
import ml_dtypes
import concourse.bass as bass
import concourse.mybir as mybir
from concourse.bass_utils import run_bass_kernel_spmd
import concourse.bass_utils as _bu

if not getattr(_bu, "_ldw_patched", False):
    _orig_run_command = _bu.run_command

    def _run_command_ldw(argv, **kw):
        argv = ["--enable-ldw-opt=true" if a == "--enable-ldw-opt=false" else a
                for a in argv]
        return _orig_run_command(argv, **kw)

    _bu.run_command = _run_command_ldw
    _bu._ldw_patched = True

F32 = mybir.dt.float32
BF16 = mybir.dt.bfloat16
AF = mybir.ActivationFunctionType
ALU = mybir.AluOpType
AX = mybir.AxisListType

N_CORES = 8
SEQ = 2048
D = 1024
NSEQ = 4
NT = SEQ // 128
NH = 16
DH = 64
NE = 8
FF = 1024
EPS = 1e-5
BETA = 0.125


def _prod(xs):
    r = 1
    for v in xs:
        r *= v
    return r


class Prog:

    CENG = ("pe", "act", "dve")

    def __init__(self, nc, ctx):
        self.nc = nc
        self.ctx = ctx
        self.stream = {e: [] for e in ("pe", "act", "dve", "pool", "sync")}
        self.sems = {}
        self.cnt = {}
        self.waited = {e: {} for e in self.stream}
        self.track = {}
        self.engsem = {}
        self.dmasem = {}
        self.psum_names = set()
        self.arena_names = set()
        self.last_barrier = []
        self.out_waits = {}
        self.dma_lanes = {}
        self.dma_lane_sems = set()

    def _mksem(self, name):
        self.sems[name] = self.ctx.enter_context(self.nc.semaphore(name))
        self.cnt[name] = 0
        return name

    def new_tag(self, tag):
        for e in ("pe", "act", "dve", "pool"):
            self.engsem[e] = self._mksem(f"{e}_{tag}")

    def lane(self, key):
        if key not in self.dma_lanes:
            name = self._mksem(f"dl_{key}")
            self.dma_lanes[key] = name
            self.dma_lane_sems.add(name)
        return self.dma_lanes[key]

    def _box(self, ap):
        t = ap.tensor
        name = t.name
        if name in self.psum_names:
            dims = [list(d) for d in ap.ap]
            off = ap.offset
            shape = list(t.shape)
            F = _prod(shape[1:]) if len(shape) > 1 else 1
            rem = off % F if F else 0
            lo = hi = rem
            for s, c in dims[1:]:
                d = s * (c - 1)
                if d >= 0:
                    hi += d
                else:
                    lo += d
            b0 = (lo * 4) // 2048
            b1 = ((hi * 4) // 2048) + 1
            return (name, 0, 1, b0, b1)
        dims = [list(d) for d in ap.ap]
        off = ap.offset
        tname = type(t).__name__
        if "DRam" in tname:
            lo = hi = off
            for s, c in dims:
                d = s * (c - 1)
                if d >= 0:
                    hi += d
                else:
                    lo += d
            return (name, 0, 1, lo, hi + 1)
        shape = list(t.shape)
        F = _prod(shape[1:]) if len(shape) > 1 else 1
        s0, c0 = dims[0]
        ok = F > 0 and s0 != 0 and s0 % F == 0
        if ok:
            for s, c in dims[1:]:
                if c > 1 and abs(s) * (c - 1) >= F:
                    ok = False
                    break
        if not ok:
            return (name, 0, 1 << 20, 0, 1 << 40)
        p0 = off // F
        pstep = s0 // F
        p1 = p0 + pstep * (c0 - 1) + 1
        rem = off % F
        lo = hi = rem
        for s, c in dims[1:]:
            d = s * (c - 1)
            if d >= 0:
                hi += d
            else:
                lo += d
        return (name, p0, p1, lo, hi + 1)

    @staticmethod
    def _ovl(a, b):
        return a[0] == b[0] and not (
            a[2] <= b[1] or b[2] <= a[1] or a[4] <= b[3] or b[4] <= a[3]
        )

    @staticmethod
    def _covers(a, b):
        return (
            a[0] == b[0]
            and a[1] <= b[1]
            and a[2] >= b[2]
            and a[3] <= b[3]
            and a[4] >= b[4]
        )

    def emit(self, eng, fn, reads=(), writes=(), dma=None, out_dma=False):
        waits = {}

        def need(sem, val):
            if sem in self.dma_lane_sems:
                val = self.cnt[sem]
            if val <= 0:
                return
            if self.waited[eng].get(sem, 0) >= val:
                return
            if waits.get(sem, 0) < val:
                waits[sem] = val

        rb = [self._box(ap) for ap in reads]
        wb = [self._box(ap) for ap in writes]
        for b in rb:
            ent = self.track.get(b[0])
            if ent:
                for ob, sem, val in ent["w"]:
                    if self._ovl(b, ob):
                        need(sem, val)
                if b[0] in self.psum_names:
                    for ob, sem, val in ent["r"]:
                        if self._ovl(b, ob):
                            need(sem, val)
        for b in wb:
            ent = self.track.get(b[0])
            if ent:
                for ob, sem, val in ent["w"]:
                    if self._ovl(b, ob):
                        need(sem, val)
                for ob, sem, val in ent["r"]:
                    if self._ovl(b, ob):
                        need(sem, val)
        if dma and any(b[0] in self.arena_names for b in wb):
            for sem, val in self.last_barrier:
                need(sem, val)
        semname = self.lane(dma) if dma else self.engsem[eng]
        amt = 16 if dma else 1
        self.cnt[semname] += amt
        val = self.cnt[semname]
        wl = sorted(waits.items())
        for s, v in wl:
            self.waited[eng][s] = max(self.waited[eng].get(s, 0), v)
        self.stream[eng].append((wl, fn, semname, amt))
        if out_dma:
            self.out_waits[semname] = val
        for b in wb:
            ent = self.track.setdefault(b[0], {"w": [], "r": []})
            ent["w"] = [e_ for e_ in ent["w"] if not self._covers(b, e_[0])]
            ent["r"] = [e_ for e_ in ent["r"] if not self._covers(b, e_[0])]
            ent["w"].append((b, semname, val))
        for b in rb:
            ent = self.track.setdefault(b[0], {"w": [], "r": []})
            ent["r"].append((b, semname, val))
        return val

    def emit_raw(self, eng, fn):
        semname = self.engsem[eng]
        self.cnt[semname] += 1
        self.stream[eng].append(([], fn, semname, 1))

    def barrier(self):
        snap = [(self.engsem[e], self.cnt[self.engsem[e]]) for e in self.CENG]
        snap = [(s, v) for s, v in snap if v > 0]
        for e in self.CENG:
            wl = []
            for s, v in snap:
                if self.waited[e].get(s, 0) < v and s != self.engsem[e]:
                    wl.append((s, v))
                    self.waited[e][s] = v
            if wl:
                self.stream[e].append((wl, None, None, 0))
        self.last_barrier = snap

    def final_waits(self):
        per_eng = {}
        for semname, val in self.out_waits.items():
            per_eng.setdefault("pool", []).append((semname, val))
        for e, wl in per_eng.items():
            wl = [(s, v) for s, v in wl if self.waited[e].get(s, 0) < v]
            if wl:
                self.stream[e].append((sorted(wl), None, None, 0))

    def install(self):
        def runner(eng):
            def _run(e):
                for wl, fn, semname, amt in self.stream[eng]:
                    for s, v in wl:
                        e.wait_ge(self.sems[s], v)
                    if fn is not None:
                        inst = fn(e)
                        inst.then_inc(self.sems[semname], amt)
            return _run

        with self.nc.Block() as block:
            block.tensor(runner("pe"))
            block.scalar(runner("act"))
            block.vector(runner("dve"))
            block.gpsimd(runner("pool"))
            block.sync(runner("sync"))


def build_program(n_seq=NSEQ, stop=99):
    nc = bass.Bass("TRN2", target_bir_lowering=False)

    x_h = nc.dram_tensor("x", [n_seq, SEQ, D], F32, kind="ExternalInput")
    out_h = nc.dram_tensor("out", [n_seq, SEQ, D], F32, kind="ExternalOutput")
    wq_h = nc.dram_tensor("wq_pre", [8, 128, 8, 128], F32, kind="ExternalInput")
    wk_h = nc.dram_tensor("wk_pre", [8, 128, 8, 128], F32, kind="ExternalInput")
    wv_h = nc.dram_tensor("wv", [D, D], F32, kind="ExternalInput")
    wo_h = nc.dram_tensor("wo", [D, D], F32, kind="ExternalInput")
    bo_h = nc.dram_tensor("bo", [1, D], F32, kind="ExternalInput")
    gw_h = nc.dram_tensor("gw_pre", [128, 8, 8], F32, kind="ExternalInput")
    gb_h = nc.dram_tensor("gb", [1, 8], F32, kind="ExternalInput")
    w1_h = nc.dram_tensor("w1_pre", [9, 8, 128, 8, 2, 128], F32, kind="ExternalInput")
    b1_h = nc.dram_tensor("b1r", [144, 128], F32, kind="ExternalInput")
    w2_h = nc.dram_tensor("w2_pre", [9, 8, 128, D], F32, kind="ExternalInput")
    rb2_h = nc.dram_tensor("rb2", [8, D], F32, kind="ExternalInput")
    sb2_h = nc.dram_tensor("sb2", [1, D], F32, kind="ExternalInput")

    ident_np = np.eye(128, dtype=np.float32)
    tri = np.zeros((128, 4, 512), dtype=ml_dtypes.bfloat16)
    for k in range(4):
        for p_ in range(128):
            q0 = k * 128 + p_
            tri[p_, k, q0:] = 1.0
    tri = tri.reshape(128, 2048)
    ones128_f = np.ones((1, 128), dtype=np.float32)
    ones128_b = np.ones((1, 128), dtype=ml_dtypes.bfloat16)
    ident_h = nc.inline_tensor(ident_np, name="identf")
    tri_h = nc.inline_tensor(tri, name="trimask")
    onesf_h = nc.inline_tensor(ones128_f, name="onesf")
    onesb_h = nc.inline_tensor(ones128_b, name="onesb")

    x = x_h.ap()
    out = out_h.ap()

    ctx = ExitStack()
    with ctx:
        RESERVE = 209920
        resv = nc.alloc_sbuf_tensor("manual_reserve", [128, RESERVE], mybir.dt.uint8)
        base = None
        for al in nc.m.functions[0].allocations:
            if isinstance(al, mybir.MemoryLocationSet) and al.memorylocations and                al.memorylocations[0].name == resv.name:
                base = al.memorylocations[0].addr
        assert base is not None
        off = [0]

        def sb(name, shape, dtype, align=64):
            nbytes = _prod(shape[1:]) * (2 if dtype == BF16 else 4)
            o = (off[0] + align - 1) // align * align
            t = nc.alloc_sbuf_tensor_at(name, shape, dtype, offset=base + o)
            off[0] = o + nbytes
            return t

        acc = sb("acc", [128, NT, D], F32)
        xnT = sb("xnT", [128, 8, SEQ], BF16)
        identf = sb("identf_sb", [128, 128], F32)
        trimask = sb("trimask_sb", [128, 2048], BF16)
        onesf = sb("onesf_sb", [1, 128], F32)
        onesb = sb("onesb_sb", [1, 128], BF16)
        biasT = sb("biasT", [128, 144], F32)
        rb2_sb = sb("rb2_sb", [8, D], BF16)
        bo_sb = sb("bo_sb", [1, D], BF16)
        sb2_sb = sb("sb2_sb", [1, D], BF16)
        gw_sb = sb("gw_sb", [128, 8, 8], F32)
        gb_sb = sb("gb_sb", [1, 8], F32)
        cw = sb("cw", [128, NT, 8], F32)
        cwT = sb("cwT", [8, SEQ], BF16)
        stats = sb("stats", [128, 128], F32)
        epsc = sb("epsc", [128, 1], F32)
        rrow = sb("rrow", [1, 512], F32)
        xnf = sb("xnf", [128, 3, 128], F32)
        wqb = sb("wqb", [128, 2, 8, 128], BF16)
        wkb = sb("wkb", [128, 2, 8, 128], BF16)

        arena0 = off[0]
        off[0] = arena0
        V_sb = sb("V_sb", [128, NT, NH, 65], BF16)
        QT = sb("QT", [128, SEQ], BF16)
        KT = sb("KT", [128, SEQ], BF16)
        attn_c = off[0]
        wv_sb = sb("wv_sb", [128, 8, D], BF16)
        xnrow = sb("xnrow", [128, D], F32)
        sq = sb("sq", [128, D], F32)
        b1raw = sb("b1raw", [128, 128], F32)
        b1raw2 = sb("b1raw2", [16, 128], F32)
        glog = sb("glog", [128, NT, 8], F32)
        top2 = sb("top2", [128, 6, NT, 8], F32)
        off[0] = attn_c
        AVT = sb("AVT", [128, 4, SEQ], BF16)
        expT = sb("expT", [128, 6, 1024], BF16)
        avf = sb("avf", [64, 512], F32)
        wo_sb = sb("wo_sb", [128, 4, D], BF16)
        off[0] = arena0
        gT = sb("gT", [128, 8, SEQ], BF16)
        w2b = sb("w2b", [128, 2, 8, D], BF16)
        w1b = sb("w1b", [128, 2, 8, 2, 128], BF16)
        sa = sb("sa", [128, SEQ], F32)
        ffn_end = off[0]
        assert ffn_end <= 209920, f"sbuf overflow {ffn_end}"

        pA = nc.alloc_psum_tensor("pA", [128, 2048], F32)
        pB = nc.alloc_psum_tensor("pB", [128, 2048], F32)

        p = Prog(nc, ctx)
        p.psum_names = {pA.name, pB.name}
        p.arena_names = {t.name for t in [
            V_sb, QT, KT, wv_sb, xnrow, sq, b1raw, b1raw2, glog, top2,
            AVT, expT, avf, wo_sb, gT, w2b, w1b, sa,
        ]}
        p.new_tag("pro")

        E = p.emit

        def dma(eng, lane, dst, src, out_dma=False):
            E(eng, lambda e, d=dst, s=src: e.dma_start(out=d, in_=s),
              reads=[src], writes=[dst], dma=lane, out_dma=out_dma)

        dma("sync", "const", identf[:, :], ident_h.ap())
        dma("sync", "const", trimask[:, :], tri_h.ap())
        dma("sync", "const", onesf[:, :], onesf_h.ap())
        dma("sync", "const", onesb[:, :], onesb_h.ap())
        dma("sync", "const", gw_sb[:, :, :], gw_h.ap())
        dma("sync", "const", gb_sb[:, :], gb_h.ap())
        dma("sync", "const", b1raw[:, :], b1_h.ap()[0:128, :])
        dma("sync", "const", b1raw2[:, :], b1_h.ap()[128:144, :])
        dma("pool", "bias", rb2_sb[:, :], rb2_h.ap())
        dma("pool", "bias", bo_sb[:, :], bo_h.ap())
        dma("pool", "bias", sb2_sb[:, :], sb2_h.ap())

        E("dve", lambda e: e.memset(epsc[:, :], EPS), writes=[epsc[:, :]])
        E("pe", lambda e: e.transpose(pA[0:128, 0:128], b1raw[:, :], identf[:, :]),
          reads=[b1raw[:, :], identf[:, :]], writes=[pA[0:128, 0:128]])
        E("dve", lambda e: e.tensor_copy(biasT[:, 0:128], pA[0:128, 0:128]),
          reads=[pA[0:128, 0:128]], writes=[biasT[:, 0:128]])
        E("pe", lambda e: e.transpose(pA[0:128, 512:528], b1raw2[:, :], identf[0:16, 0:16]),
          reads=[b1raw2[:, :], identf[:, :]], writes=[pA[0:128, 512:528]])
        E("dve", lambda e: e.tensor_copy(biasT[:, 128:144], pA[0:128, 512:528]),
          reads=[pA[0:128, 512:528]], writes=[biasT[:, 128:144]])

        p.barrier()

        for s in range(n_seq):
            for t in range(NT):
                dma("sync", f"xl{t}", acc[:, t, :],
                    x[s, t * 128:(t + 1) * 128, :])
            dma("pool", "wv", wv_sb[:, :, :],
                wv_h.ap().rearrange("(kc p) n -> p kc n", p=128))
            dma("pool", "wq0", wqb[:, 0, :, :], wq_h.ap()[0, :, :, :])
            dma("pool", "wk0", wkb[:, 0, :, :], wk_h.ap()[0, :, :, :])
            E("dve", lambda e: e.memset(V_sb[:, :, :, 64:65], 1.0),
              writes=[V_sb[:, :, :, 64:65]])

            s1a = stats[:, 0:16]
            s2a = stats[:, 16:32]
            ma = stats[:, 32:48]
            va = stats[:, 48:64]
            ra = stats[:, 64:80]
            mxa = stats[:, 80:96]
            for t in range(NT):
                at = acc[:, t, :]
                E("dve", lambda e, at=at, o=s1a[:, t:t + 1]: e.reduce_sum(
                    o, at, axis=AX.X), reads=[at], writes=[s1a[:, t:t + 1]])
                E("act", lambda e, at=at: e.activation(sq[:, :], at, AF.Square),
                  reads=[at], writes=[sq[:, :]])
                E("dve", lambda e, o=s2a[:, t:t + 1]: e.reduce_sum(
                    o, sq[:, :], axis=AX.X), reads=[sq[:, :]],
                  writes=[s2a[:, t:t + 1]])
            E("dve", lambda e: e.tensor_scalar(ma, s1a, 1.0 / D, None, op0=ALU.mult),
              reads=[s1a], writes=[ma])
            E("act", lambda e: e.activation(mxa, ma, AF.Square),
              reads=[ma], writes=[mxa])
            E("dve", lambda e: e.tensor_scalar(va, s2a, 1.0 / D, None, op0=ALU.mult),
              reads=[s2a], writes=[va])
            E("dve", lambda e: e.tensor_tensor(va, va, mxa, op=ALU.subtract),
              reads=[va, mxa], writes=[va])
            E("act", lambda e: e.activation(va, va, AF.Ln, bias=epsc[:, :]),
              reads=[va, epsc[:, :]], writes=[va])
            E("act", lambda e: e.activation(ra, va, AF.Exp, scale=-0.5),
              reads=[va], writes=[ra])
            for t in range(NT):
                at = acc[:, t, :]
                E("dve", lambda e, at=at, m=ma[:, t:t + 1], r=ra[:, t:t + 1]:
                  e.tensor_scalar(xnrow[:, :], at, m, r,
                                  op0=ALU.subtract, op1=ALU.mult),
                  reads=[at, ma[:, t:t + 1], ra[:, t:t + 1]], writes=[xnrow[:, :]])
                gate_ps = pA[0:128, 1024:1032]
                xps = [pA[0:128, 0:128], pA[0:128, 512:640],
                       pA[0:128, 1536:1664]]
                for kc in range(8):
                    xp = xps[kc % 3]
                    xsrc = xnrow[:, kc * 128:(kc + 1) * 128]
                    E("pe", lambda e, xp=xp, xsrc=xsrc: e.transpose(
                        xp, xsrc, identf[:, :]),
                      reads=[xsrc, identf[:, :]], writes=[xp])
                    xf = xnf[:, kc % 3, :]
                    E("dve", lambda e, xp=xp, xf=xf: e.tensor_copy(xf, xp),
                      reads=[xp], writes=[xf])
                    xdst = xnT[:, kc, t * 128:(t + 1) * 128]
                    E("act", lambda e, xf=xf, xdst=xdst: e.copy(xdst, xf),
                      reads=[xf], writes=[xdst])
                    E("pe", lambda e, xf=xf, kc=kc, gp=gate_ps: e.matmul(
                        gp, xf, gw_sb[:, kc, :], start=(kc == 0), stop=False,
                        skip_group_check=True),
                      reads=[xf, gw_sb[:, kc, :]], writes=[gate_ps])
                E("pe", lambda e, gp=gate_ps: e.matmul(
                    gp, onesf[0:1, :], gb_sb[0:1, :], start=False, stop=True,
                    skip_group_check=True),
                  reads=[onesf[0:1, :], gb_sb[0:1, :]], writes=[gate_ps])
                E("act", lambda e, gp=gate_ps, o=glog[:, t, :]: e.copy(o, gp),
                  reads=[gate_ps], writes=[glog[:, t, :]])
                vps = pB[0:128, 0:1024] if t % 2 == 0 else pB[0:128, 1024:2048]

                def vgrp(e, t=t, vps=vps):
                    for kc in range(8):
                        lhsT = xnT[:, kc, t * 128:(t + 1) * 128]
                        for nb in range(2):
                            inst = e.matmul(
                                vps[:, nb * 512:(nb + 1) * 512], lhsT,
                                wv_sb[:, kc, nb * 512:(nb + 1) * 512],
                                start=(kc == 0), stop=(kc == 7))
                    return inst

                E("pe", vgrp,
                  reads=[xnT[:, :, t * 128:(t + 1) * 128], wv_sb[:, :, :]],
                  writes=[vps])
                vdst = V_sb[:, t, :, 0:64]
                E("dve", lambda e, vps=vps, vdst=vdst: e.tensor_copy(
                    vdst, vps.rearrange("p (h d) -> p h d", h=16)),
                  reads=[vps], writes=[vdst])

            g3 = glog[:, :, :]
            probs = top2[:, 0, :, :]
            eq = top2[:, 1, :, :]
            c1 = top2[:, 2, :, :]
            c2 = top2[:, 3, :, :]
            f1 = top2[:, 4, :, :]
            w12 = top2[:, 5, :, :]
            mx3 = mxa.rearrange("p (t o) -> p t o", o=1)

            def bc16(ap2d):
                return bass.AP(ap2d.tensor, ap2d.offset,
                               [list(d) for d in ap2d.ap] + [[0, 8]])

            E("dve", lambda e: e.reduce_max(mx3, g3, axis=AX.X, negate=True),
              reads=[g3], writes=[mxa])
            E("dve", lambda e: e.tensor_tensor(probs, g3, bc16(mxa), op=ALU.add),
              reads=[g3, mxa], writes=[probs])
            E("act", lambda e: e.activation(probs, probs, AF.Exp),
              reads=[probs], writes=[probs])
            ssa = stats[:, 96:112]
            ss3 = ssa.rearrange("p (t o) -> p t o", o=1)
            E("dve", lambda e: e.reduce_sum(ss3, probs, axis=AX.X),
              reads=[probs], writes=[ssa])
            E("dve", lambda e: e.reciprocal(ssa, ssa), reads=[ssa], writes=[ssa])
            E("dve", lambda e: e.tensor_tensor(probs, probs, bc16(ssa), op=ALU.mult),
              reads=[probs, ssa], writes=[probs])

            def first_of_max(srcv, fdst):
                E("dve", lambda e: e.reduce_max(mx3, srcv, axis=AX.X),
                  reads=[srcv], writes=[mxa])
                E("dve", lambda e: e.tensor_tensor(eq, srcv, bc16(mxa), op=ALU.is_ge),
                  reads=[srcv, mxa], writes=[eq])
                E("dve", lambda e: e.tensor_copy(c1[:, :, 0:1], eq[:, :, 0:1]),
                  reads=[eq], writes=[c1])
                E("dve", lambda e: e.tensor_tensor(
                    c1[:, :, 1:8], eq[:, :, 1:8], eq[:, :, 0:7], op=ALU.add),
                  reads=[eq], writes=[c1])
                E("dve", lambda e: e.tensor_copy(c2[:, :, 0:2], c1[:, :, 0:2]),
                  reads=[c1], writes=[c2])
                E("dve", lambda e: e.tensor_tensor(
                    c2[:, :, 2:8], c1[:, :, 2:8], c1[:, :, 0:6], op=ALU.add),
                  reads=[c1], writes=[c2])
                E("dve", lambda e: e.tensor_copy(c1[:, :, 0:4], c2[:, :, 0:4]),
                  reads=[c2], writes=[c1])
                E("dve", lambda e: e.tensor_tensor(
                    c1[:, :, 4:8], c2[:, :, 4:8], c2[:, :, 0:4], op=ALU.add),
                  reads=[c2], writes=[c1])
                E("dve", lambda e: e.tensor_scalar(
                    c2, c1, 1.0, None, op0=ALU.is_equal),
                  reads=[c1], writes=[c2])
                E("dve", lambda e, fdst=fdst: e.tensor_tensor(
                    fdst, eq, c2, op=ALU.mult),
                  reads=[eq, c2], writes=[fdst])

            first_of_max(probs, f1)
            E("dve", lambda e: e.tensor_scalar(
                w12, f1, -1.0, 1.0, op0=ALU.mult, op1=ALU.add),
              reads=[f1], writes=[w12])
            E("dve", lambda e: e.tensor_tensor(c1, probs, w12, op=ALU.mult),
              reads=[probs, w12], writes=[c1])
            E("dve", lambda e: e.tensor_copy(w12, c1), reads=[c1], writes=[w12])
            f2 = top2[:, 3, :, :]
            first_of_max(w12, f2)
            E("dve", lambda e: e.tensor_tensor(f1, f1, f2, op=ALU.add),
              reads=[f1, f2], writes=[f1])
            E("dve", lambda e: e.tensor_tensor(cw[:, :, :], probs, f1, op=ALU.mult),
              reads=[probs, f1], writes=[cw[:, :, :]])
            for t in range(NT):
                cwt = cw[:, t, :]
                cwp = pA[0:8, 1536:1664]
                E("pe", lambda e, cwt=cwt, cwp=cwp: e.transpose(
                    cwp, cwt, identf[:, :]),
                  reads=[cwt, identf[:, :]], writes=[cwp])
                cwd = cwT[:, t * 128:(t + 1) * 128]
                E("dve", lambda e, cwp=cwp, cwd=cwd: e.tensor_copy(cwd, cwp),
                  reads=[cwp], writes=[cwd])

            if stop <= 1:
                continue
            p.barrier()

            if stop <= 2:
                continue
            sc = [pA[0:128, 0:1024], pA[0:128, 1024:2048], pB[0:128, 1024:2048]]
            rbp = pA[0:64, 0:512]
            wop = pB[0:128, 1024:2048]
            pend_norm = [None]
            cglob = [0]

            for hp in range(8):
                if hp + 1 < 8:
                    dma("pool", f"wq{(hp + 1) % 2}", wqb[:, (hp + 1) % 2, :, :], wq_h.ap()[hp + 1, :, :, :])
                    dma("pool", f"wk{(hp + 1) % 2}", wkb[:, (hp + 1) % 2, :, :], wk_h.ap()[hp + 1, :, :, :])
                if hp == 0:
                    dma("pool", "wo", wo_sb[:, :, :],
                        wo_h.ap().rearrange("(c p) d -> p c d", p=128)[:, 0:4, :])

                for half in range(2):
                    ps = sc[half]
                    tk0 = half * 1024

                    def qgrp(e, ps=ps, tk0=tk0, hp=hp):
                        for kc in range(8):
                            lhsT = wqb[:, hp % 2, kc, :]
                            for nb in range(2):
                                a = tk0 + nb * 512
                                inst = e.matmul(
                                    ps[:, nb * 512:(nb + 1) * 512], lhsT,
                                    xnT[:, kc, a:a + 512],
                                    start=(kc == 0), stop=(kc == 7))
                        return inst

                    E("pe", qgrp,
                      reads=[wqb[:, hp % 2, :, :], xnT[:, :, tk0:tk0 + 1024]],
                      writes=[ps])
                    qd = QT[:, tk0:tk0 + 1024]
                    E("dve", lambda e, ps=ps, qd=qd: e.tensor_scalar(
                        qd, ps, BETA, None, op0=ALU.mult),
                      reads=[ps], writes=[qd])

                    def kgrp(e, ps=ps, tk0=tk0, hp=hp):
                        for kc in range(8):
                            lhsT = wkb[:, hp % 2, kc, :]
                            for nb in range(2):
                                a = tk0 + nb * 512
                                inst = e.matmul(
                                    ps[:, nb * 512:(nb + 1) * 512], lhsT,
                                    xnT[:, kc, a:a + 512],
                                    start=(kc == 0), stop=(kc == 7))
                        return inst

                    E("pe", kgrp,
                      reads=[wkb[:, hp % 2, :, :], xnT[:, :, tk0:tk0 + 1024]],
                      writes=[ps])
                    kd = KT[:, tk0:tk0 + 1024]
                    E("dve", lambda e, ps=ps, kd=kd: e.tensor_copy(kd, ps),
                      reads=[ps], writes=[kd])

                h0 = 2 * hp
                for b in range(4):
                    q0 = b * 512
                    nchunks = 4 * (b + 1)
                    av0 = pB[0:65, 0:512]
                    av1 = pB[0:65, 512:1024]
                    pend_q = []
                    for c in range(nchunks):
                        cg = cglob[0]
                        cglob[0] += 1
                        ps = sc[cg % 3]
                        eb = expT[:, cg % 6, :]
                        if c == 0 and pend_norm[0] is not None:
                            pend_norm[0]()
                            pend_norm[0] = None

                        def sgrp(e, c=c, ps=ps, q0=q0):
                            e.matmul(ps[:, 0:512],
                                     KT[0:64, c * 128:(c + 1) * 128],
                                     QT[0:64, q0:q0 + 512],
                                     start=True, stop=True,
                                     skip_group_check=True)
                            return e.matmul(ps[:, 512:1024],
                                            KT[64:128, c * 128:(c + 1) * 128],
                                            QT[64:128, q0:q0 + 512],
                                            start=True, stop=True,
                                            skip_group_check=True)

                        E("pe", sgrp,
                          reads=[KT[:, c * 128:(c + 1) * 128], QT[:, q0:q0 + 512]],
                          writes=[ps])
                        E("act", lambda e, ps=ps, eb=eb: e.activation(
                            eb, ps, AF.Exp),
                          reads=[ps], writes=[eb])
                        if c >= 4 * b:
                            koff = (c - 4 * b) * 512
                            mk = trimask[:, koff:koff + 512]
                            E("dve", lambda e, eb=eb, mk=mk: e.tensor_tensor(
                                eb[:, 0:512], eb[:, 0:512], mk, op=ALU.mult),
                              reads=[eb, mk], writes=[eb])
                            E("dve", lambda e, eb=eb, mk=mk: e.tensor_tensor(
                                eb[:, 512:1024], eb[:, 512:1024], mk, op=ALU.mult),
                              reads=[eb, mk], writes=[eb])
                        if len(pend_q) >= 2:
                            pend_q.pop(0)()
                        for _h in range(0):
                            E("pe", lambda e: e.matmul(
                                pB[0:128, 1024:1536], trimask[:, 0:128],
                                trimask[:, 0:512], start=True, stop=True,
                                skip_group_check=True),
                              writes=[pB[0:128, 1024:1536]])

                        def avgrp(e, c=c, eb=eb, av0=av0, av1=av1, h0=h0,
                                  last=(c == nchunks - 1)):
                            e.matmul(av0, V_sb[:, c, h0, :], eb[:, 0:512],
                                     start=(c == 0), stop=last,
                                     skip_group_check=True)
                            return e.matmul(av1, V_sb[:, c, h0 + 1, :],
                                            eb[:, 512:1024],
                                            start=(c == 0), stop=last,
                                            skip_group_check=True)

                        def mk_pend(c=c, eb=eb, av0=av0, av1=av1, fn=avgrp):
                            def run():
                                E("pe", fn,
                                  reads=[V_sb[:, c, h0, :], V_sb[:, c, h0 + 1, :],
                                         eb],
                                  writes=[av0, av1])
                            return run

                        pend_q.append(mk_pend())
                    for fn_ in pend_q:
                        fn_()
                    pend_q = []

                    def mk_norm(av0=av0, av1=av1, hp=hp, q0=q0):
                        def run():
                            for hh in range(2):
                                ap_ = av0 if hh == 0 else av1
                                r0 = hh * 64
                                E("act", lambda e, ap_=ap_: e.activation(
                                    rrow[0:1, :], ap_[64:65, :], AF.Ln),
                                  reads=[ap_], writes=[rrow[0:1, :]])
                                E("act", lambda e: e.activation(
                                    rrow[0:1, :], rrow[0:1, :], AF.Exp,
                                    scale=-1.0),
                                  reads=[rrow[0:1, :]], writes=[rrow[0:1, :]])
                                E("dve", lambda e, ap_=ap_: e.tensor_copy(
                                    avf[:, :], ap_[0:64, :]),
                                  reads=[ap_], writes=[avf[:, :]])
                                E("pe", lambda e: e.matmul(
                                    rbp, onesf[0:1, 0:64], rrow[0:1, :],
                                    start=True, stop=True,
                                    skip_group_check=True),
                                  reads=[onesf[0:1, 0:64], rrow[0:1, :]],
                                  writes=[rbp])
                                avtd = AVT[r0:r0 + 64, hp % 4, q0:q0 + 512]
                                E("dve", lambda e, avtd=avtd: e.tensor_tensor(
                                    avtd, avf[:, :], rbp, op=ALU.mult),
                                  reads=[avf[:, :], rbp], writes=[avtd])
                        return run

                    pend_norm[0] = mk_norm()

                if hp == 3 or hp == 7:
                    if pend_norm[0] is not None:
                        pend_norm[0]()
                        pend_norm[0] = None
                    first = (hp == 3)
                    for t in range(NT):
                        def wogrp(e, t=t, first=first):
                            inst = None
                            for i in range(4):
                                lhsT = AVT[:, i, t * 128:(t + 1) * 128]
                                last_i = (i == 3) and not first
                                for nb in range(2):
                                    inst = e.matmul(
                                        wop[:, nb * 512:(nb + 1) * 512], lhsT,
                                        wo_sb[:, i, nb * 512:(nb + 1) * 512],
                                        start=(i == 0),
                                        stop=(i == 3 and not first),
                                        skip_group_check=True)
                            if first:
                                for nb in range(2):
                                    inst = e.matmul(
                                        wop[:, nb * 512:(nb + 1) * 512],
                                        onesb[0:1, :],
                                        bo_sb[0:1, nb * 512:(nb + 1) * 512],
                                        start=False, stop=True,
                                        skip_group_check=True)
                            return inst

                        E("pe", wogrp,
                          reads=[AVT[:, :, t * 128:(t + 1) * 128],
                                 wo_sb[:, :, :], onesb[0:1, :], bo_sb[0:1, :]],
                          writes=[wop])
                        at = acc[:, t, :]
                        E("dve", lambda e, at=at: e.tensor_tensor(
                            at, at, wop, op=ALU.add),
                          reads=[at, wop], writes=[at])
                    if hp == 3:
                        dma("pool", "wo", wo_sb[:, :, :],
                            wo_h.ap().rearrange("(c p) d -> p c d", p=128)[:, 4:8, :])

            p.barrier()

            if stop <= 3:
                continue
            dma("pool", "w2_0", w2b[:, 0, :, :],
                w2_h.ap()[0, :, :, :].rearrange("c p d -> p c d"))
            dma("pool", "w1_0", w1b[:, 0, :, :, :], w1_h.ap()[0, 0, :, :, :, :])
            for ex in range(9):
                if ex + 1 < 9:
                    dma("pool", f"w2_{(ex + 1) % 2}", w2b[:, (ex + 1) % 2, :, :],
                        w2_h.ap()[ex + 1, :, :, :].rearrange("c p d -> p c d"))
                bcol0 = 128 if ex == 0 else (ex - 1) * 16
                for pr in range(8):
                    nxt = (ex, pr + 1) if pr + 1 < 8 else (ex + 1, 0)
                    if nxt[0] < 9:
                        dma("pool", f"w1_{(pr + 1) % 2}", w1b[:, (pr + 1) % 2, :, :, :],
                            w1_h.ap()[nxt[0], nxt[1], :, :, :, :])

                    def agrp(e, pr=pr, j=0):
                        for kc in range(8):
                            lhsT = w1b[:, pr % 2, kc, 0, :]
                            for nb in range(4):
                                inst = e.matmul(
                                    pA[0:128, nb * 512:(nb + 1) * 512], lhsT,
                                    xnT[:, kc, nb * 512:(nb + 1) * 512],
                                    start=(kc == 0), stop=(kc == 7))
                        return inst

                    E("pe", agrp,
                      reads=[w1b[:, pr % 2, :, 0, :], xnT[:, :, :]],
                      writes=[pA[0:128, 0:2048]])
                    bc_a = biasT[:, bcol0 + pr:bcol0 + pr + 1]
                    E("act", lambda e, bc_a=bc_a: e.activation(
                        sa[:, :], pA[0:128, 0:2048], AF.Silu, bias=bc_a),
                      reads=[pA[0:128, 0:2048], bc_a], writes=[sa[:, :]])

                    def bgrp(e, pr=pr):
                        for kc in range(8):
                            lhsT = w1b[:, pr % 2, kc, 1, :]
                            for nb in range(4):
                                inst = e.matmul(
                                    pB[0:128, nb * 512:(nb + 1) * 512], lhsT,
                                    xnT[:, kc, nb * 512:(nb + 1) * 512],
                                    start=(kc == 0), stop=(kc == 7))
                        return inst

                    E("pe", bgrp,
                      reads=[w1b[:, pr % 2, :, 1, :], xnT[:, :, :]],
                      writes=[pB[0:128, 0:2048]])
                    bc_b = biasT[:, bcol0 + pr + 8:bcol0 + pr + 9]
                    gd = gT[:, pr, :]
                    E("dve", lambda e, bc_b=bc_b, gd=gd: e.scalar_tensor_tensor(
                        gd, pB[0:128, 0:2048], bc_b, sa[:, :],
                        op0=ALU.add, op1=ALU.mult),
                      reads=[pB[0:128, 0:2048], bc_b, sa[:, :]], writes=[gd])

                for t in range(NT):
                    wp = (pA if t % 2 == 0 else pB)[0:128, 0:1024]

                    def w2grp(e, t=t, ex=ex, wp=wp):
                        inst = None
                        for fc in range(8):
                            lhsT = gT[:, fc, t * 128:(t + 1) * 128]
                            for nb in range(2):
                                inst = e.matmul(
                                    wp[:, nb * 512:(nb + 1) * 512], lhsT,
                                    w2b[:, ex % 2, fc, nb * 512:(nb + 1) * 512],
                                    start=(fc == 0),
                                    stop=(fc == 7 and ex != 0),
                                    skip_group_check=True)
                        if ex == 0:
                            for nb in range(2):
                                e.matmul(
                                    wp[:, nb * 512:(nb + 1) * 512],
                                    onesb[0:1, :],
                                    sb2_sb[0:1, nb * 512:(nb + 1) * 512],
                                    start=False, stop=False,
                                    skip_group_check=True)
                                inst = e.matmul(
                                    wp[:, nb * 512:(nb + 1) * 512],
                                    cwT[0:8, t * 128:(t + 1) * 128],
                                    rb2_sb[0:8, nb * 512:(nb + 1) * 512],
                                    start=False, stop=True,
                                    skip_group_check=True)
                        return inst

                    reads = [gT[:, :, t * 128:(t + 1) * 128], w2b[:, ex % 2, :, :]]
                    if ex == 0:
                        reads += [onesb[0:1, :], sb2_sb[0:1, :],
                                  cwT[0:8, t * 128:(t + 1) * 128], rb2_sb[0:8, :]]
                    E("pe", w2grp, reads=reads, writes=[wp])
                    at = acc[:, t, :]
                    if ex == 0:
                        E("dve", lambda e, at=at, wp=wp: e.tensor_tensor(
                            at, at, wp, op=ALU.add),
                          reads=[at, wp], writes=[at])
                    else:
                        cwc = cw[:, t, ex - 1:ex]
                        E("dve", lambda e, at=at, wp=wp, cwc=cwc:
                          e.scalar_tensor_tensor(
                              at, wp, cwc, at, op0=ALU.mult, op1=ALU.add),
                          reads=[at, wp, cwc], writes=[at])
                    if ex == 8:
                        dma("pool", f"st{t}", out[s, t * 128:(t + 1) * 128, :],
                            acc[:, t, :], out_dma=True)

            p.barrier()

        p.final_waits()
        build_program.last_prog = p
        p.install()
    return nc


_PROG_CACHE = {}


def _get_prog(n_seq=NSEQ):
    if n_seq not in _PROG_CACHE:
        _PROG_CACHE[n_seq] = build_program(n_seq)
    return _PROG_CACHE[n_seq]


def _prep_weights(inputs):
    f32 = np.float32
    wq = np.ascontiguousarray(inputs["L0_attn_Wq"], f32)
    wk = np.ascontiguousarray(inputs["L0_attn_Wk"], f32)
    wv = np.ascontiguousarray(inputs["L0_attn_Wv"], f32)
    wo = np.ascontiguousarray(inputs["L0_attn_Wo"], f32)
    bo = np.ascontiguousarray(inputs["L0_attn_bo"], f32).reshape(1, D)
    gw = np.ascontiguousarray(inputs["L0_ffn_gate_w"], f32)
    gb = np.ascontiguousarray(inputs["L0_ffn_gate_b"], f32).reshape(1, 8)
    sw1 = np.ascontiguousarray(inputs["L0_ffn_sw1"], f32)
    sb1 = np.ascontiguousarray(inputs["L0_ffn_sb1"], f32)
    sw2 = np.ascontiguousarray(inputs["L0_ffn_sw2"], f32)
    sb2 = np.ascontiguousarray(inputs["L0_ffn_sb2"], f32)
    rw1 = np.ascontiguousarray(inputs["L0_ffn_rw1"], f32)
    rb1 = np.ascontiguousarray(inputs["L0_ffn_rb1"], f32)
    rw2 = np.ascontiguousarray(inputs["L0_ffn_rw2"], f32)
    rb2 = np.ascontiguousarray(inputs["L0_ffn_rb2"], f32)

    wq_pre = np.ascontiguousarray(
        wq.reshape(8, 128, 8, 128).transpose(2, 1, 0, 3))
    wk_pre = np.ascontiguousarray(
        wk.reshape(8, 128, 8, 128).transpose(2, 1, 0, 3))
    gw_pre = np.ascontiguousarray(gw.reshape(8, 128, 8).transpose(1, 0, 2))
    w1_all = np.concatenate([sw1, rw1], axis=0)
    w1_pre = np.ascontiguousarray(
        w1_all.reshape(9, 8, 128, 2, 8, 128).transpose(0, 4, 2, 1, 3, 5))
    b1 = np.concatenate(
        [rb1.reshape(128, 128), sb1.reshape(16, 128)], axis=0)
    b1 = np.ascontiguousarray(b1)
    w2_all = np.concatenate([sw2, rw2], axis=0).reshape(9, 1024, D)
    w2_pre = np.ascontiguousarray(
        w2_all.reshape(9, 8, 128, D))
    sb2r = np.ascontiguousarray(sb2.reshape(1, D))
    rb2c = np.ascontiguousarray(rb2)
    return dict(wq_pre=wq_pre, wk_pre=wk_pre, wv=wv, wo=wo, bo=bo,
                gw_pre=gw_pre, gb=gb, w1_pre=w1_pre, b1r=b1,
                w2_pre=w2_pre, rb2=rb2c, sb2=sb2r)


def kernel(**inputs):
    x = np.ascontiguousarray(inputs["L0_attn_x"], np.float32)
    w = _prep_weights(inputs)
    nc = _get_prog(NSEQ)
    in_maps = []
    for c in range(N_CORES):
        m = dict(w)
        m["x"] = np.ascontiguousarray(x[c * NSEQ:(c + 1) * NSEQ])
        in_maps.append(m)
    res = run_bass_kernel_spmd(nc, in_maps, core_ids=list(range(N_CORES)))
    outs = [res.results[c]["out"] for c in range(N_CORES)]
    return np.concatenate(outs, axis=0)
